# version 18
# baseline (speedup 1.0000x reference)
import contextlib

import numpy as np

import concourse.bass as bass
import concourse.mybir as mybir
from concourse.bass_utils import run_bass_kernel_spmd

F32 = mybir.dt.float32
BF16 = mybir.dt.bfloat16
AX = mybir.AxisListType.X
OP = mybir.AluOpType
AF = mybir.ActivationFunctionType

NCORES = 8
BPC = 2
S = 4096
D = 1024
H = 16
EXP = 8
KSEL = 2048.0
NG = 8
NTG = 4
NT = 32
NC = 8
ITERS = 46
NP = 4
ITERS4 = 20
KVOFF = 5 * 512


def kvcol(h):
    return (h // 7) * 512 + (h % 7) * 65


class Prog:
    def __init__(self):
        self.ops = []
        self.cnt = {}
        self.h = {}

    def sem(self, nc, stack, *names):
        for n in names:
            self.h[n] = stack.enter_context(nc.semaphore(n))
            self.cnt[n] = 0

    def at(self, name):
        return (name, self.cnt[name])

    SELF_SEM = {"vector": "vp", "scalar": "sp"}

    def add(self, eng, fn, waits=(), inc=None, dma=False):
        waits = [w for w in waits if w is not None]
        incs = []
        self_sem = self.SELF_SEM.get(eng)
        if self_sem is not None and not dma:
            waits.append((self_sem, self.cnt[self_sem]))
            if inc is None:
                inc = self_sem
            assert inc == self_sem, (eng, inc)
        if inc is not None:
            delta = 16 if dma else 1
            self.cnt[inc] += delta
            incs.append((inc, delta))
        self.ops.append((eng, fn, waits, incs))

    def dma(self, eng, outap, inap, sem, waits=(), slow=False):
        def fn(e, o=outap, i=inap, s=slow):
            return e.dma_start(out=o, in_=i, allow_slow_non_contiguous=s)
        self.add(eng, fn, waits=waits, inc=sem, dma=True)

    def emit(self, name, eng):
        for (e, fn, waits, incs) in self.ops:
            if e != name:
                continue
            for (s, v) in waits:
                if v > 0:
                    eng.wait_ge(self.h[s], v)
            inst = fn(eng)
            for (s, v) in incs:
                inst.then_inc(self.h[s], v)


def build_nc(phases="full"):
    nc = bass.Bass("TRN2", target_bir_lowering=False, detect_race_conditions=False)

    x = nc.dram_tensor("x", [BPC, S, D], F32, kind="ExternalInput").ap()
    wq_d = nc.dram_tensor("wq_d", [128, NC, D], F32, kind="ExternalInput").ap()
    wk_d = nc.dram_tensor("wk_d", [128, NC, D], F32, kind="ExternalInput").ap()
    wv_d = nc.dram_tensor("wv_d", [128, NC, D], F32, kind="ExternalInput").ap()
    wo_d = nc.dram_tensor("wo_d", [128, NC, D], F32, kind="ExternalInput").ap()
    w1_d = nc.dram_tensor("w1_d", [128, NC, EXP, 256], F32, kind="ExternalInput").ap()
    w2_d = nc.dram_tensor("w2_d", [128, 2, EXP, D], F32, kind="ExternalInput").ap()
    bqf_d = nc.dram_tensor("bqf_d", [128, NC], F32, kind="ExternalInput").ap()
    bqr_d = nc.dram_tensor("bqr_d", [1, D], F32, kind="ExternalInput").ap()
    b1f_d = nc.dram_tensor("b1f_d", [128, 2, EXP], F32, kind="ExternalInput").ap()
    bk_d = nc.dram_tensor("bk_d", [1, D], F32, kind="ExternalInput").ap()
    bv_d = nc.dram_tensor("bv_d", [1, D], F32, kind="ExternalInput").ap()
    bo_d = nc.dram_tensor("bo_d", [1, D], F32, kind="ExternalInput").ap()
    b2_d = nc.dram_tensor("b2_d", [EXP, D], F32, kind="ExternalInput").ap()
    wg_d = nc.dram_tensor("wg_d", [128, NC, EXP], F32, kind="ExternalInput").ap()
    bg_d = nc.dram_tensor("bg_d", [1, EXP], F32, kind="ExternalInput").ap()
    tri_d = nc.dram_tensor("tri_d", [128, 128], F32, kind="ExternalInput").ap()
    e16_d = nc.dram_tensor("e16_d", [16, 8, 128], F32, kind="ExternalInput").ap()
    id_d = nc.dram_tensor("id_d", [128, 128], F32, kind="ExternalInput").ap()
    onq_d = nc.dram_tensor("onq_d", [128, 128], F32, kind="ExternalInput").ap()
    ck4_d = nc.dram_tensor("ck4_d", [1, 64], F32, kind="ExternalInput").ap()
    hm_d = nc.dram_tensor("hm_d", [128, NC, H], F32, kind="ExternalInput").ap()
    out = nc.dram_tensor("out", [BPC, S, D], F32, kind="ExternalOutput").ap()

    xn_sp = nc.dram_tensor("xn_sp", [BPC, NG, 128, NC, 512], BF16).ap()
    q_sp = nc.dram_tensor("q_sp", [BPC, NG, 128, NC, 512], BF16).ap()
    sel_dr = nc.dram_tensor("sel_dr", [BPC, EXP, S], BF16).ap()
    cc_i = nc.dram_tensor("cc_i", [128, 256], F32).ap()
    cc_o = nc.dram_tensor("cc_o", [128, 256], F32).ap()

    p = Prog()
    with contextlib.ExitStack() as stack:
        Etx = stack.enter_context

        WA = Etx(nc.sbuf_tensor("WA", [128, 40960], BF16))
        wq_b = WA[:, 0:8192].rearrange("p (c n) -> p c n", c=NC)
        wk_b = WA[:, 8192:16384].rearrange("p (c n) -> p c n", c=NC)
        wv_b = WA[:, 16384:24576].rearrange("p (c n) -> p c n", c=NC)
        wo_b = WA[:, 0:8192].rearrange("p (c n) -> p c n", c=NC)
        w1_b = WA[:, 8192:24576].rearrange("p (c e n) -> p c e n", c=NC, e=EXP)
        w2_b = WA[:, 24576:40960].rearrange("p (h e n) -> p h e n", h=2, e=EXP)

        UN = Etx(nc.sbuf_tensor("UN", [128, 6144], F32))
        UNbf = UN[:, :].bitcast(BF16)
        XL = [UN[:, 0:1024], UN[:, 1024:2048], UN[:, 2048:3072], UN[:, 3072:4096]]
        ATT = UNbf[:, 0:4096].rearrange("p (c j) -> p c j", c=NC)
        XRS = [UN[:, 2048:3072], UN[:, 3072:4096]]
        OUS = [UN[:, 4096:5120], UN[:, 5120:6144]]
        XNF = [Etx(nc.sbuf_tensor(f"XNF{i}", [128, NC, 512], BF16)) for i in range(2)]
        XTM = [WA[:, 24576:25600], WA[:, 25600:26624]]
        SCA = Etx(nc.sbuf_tensor("SCA", [128, 4096], F32))
        SCAbf = SCA[:, :].bitcast(BF16)
        SCR = SCAbf[:, 0:1024]
        SCE = SCAbf[:, 1024:2048]
        SCM = SCA[:, 1024:2048]
        SCM = SCA[:, 2048:3072]
        SQS = SCA[:, 1024:2048]
        MSK = [SCAbf[:, 0:4096].rearrange("p (e j) -> p e j", e=EXP),
               SCAbf[:, 4096:8192].rearrange("p (e j) -> p e j", e=EXP)]
        CSC = Etx(nc.sbuf_tensor("CSC", [128, 1024], BF16))
        SCG = CSC[:, 0:512]
        ABUF = Etx(nc.sbuf_tensor("ABUF", [128, 8192], BF16))
        KSB = [ABUF[:, 0:1024], ABUF[:, 1024:2048]]
        VSB = [ABUF[:, 2048:3088].rearrange("p (h j) -> p h j", h=H),
               ABUF[:, 3088:4128].rearrange("p (h j) -> p h j", h=H)]
        HSB = ABUF[:, 0:8192].rearrange("p (e h j) -> p e h j", e=EXP, h=2)
        QSL = [Etx(nc.sbuf_tensor(f"QSL{i}", [128, NC, 512], BF16)) for i in range(2)]
        QSB = QSL[0][:, :, :]
        ZRS = [CSC[:, 0:512], CSC[:, 512:1024]]
        KVB = Etx(nc.sbuf_tensor("KVB", [128, BPC, NC, 128], BF16))
        KST = Etx(nc.sbuf_tensor("KST", [128, BPC, NC, H], BF16))
        KVT = WA[:, 26624:27664]
        KSM = Etx(nc.sbuf_tensor("KSM", [128, H], BF16))
        HMC = Etx(nc.sbuf_tensor("HMC", [128, NC, H], BF16))
        Z2 = Etx(nc.sbuf_tensor("Z2", [16, 512], F32))
        ZBF = Etx(nc.sbuf_tensor("ZBF", [16, 512], BF16))
        SELT = Etx(nc.sbuf_tensor("SELT", [64, S], BF16))

        ST4 = Etx(nc.sbuf_tensor("ST4", [128, 16], F32))
        STA = Etx(nc.sbuf_tensor("STA", [128, 4], F32))
        STB = Etx(nc.sbuf_tensor("STB", [128, 4], F32))
        GL = Etx(nc.sbuf_tensor("GL", [128, 256], F32))
        TKAR = Etx(nc.sbuf_tensor("TKAR", [128, 2560], F32))
        GS = TKAR[:, 0:256]
        GEQ = TKAR[:, 256:512]
        GCA = TKAR[:, 512:768]
        GCB = TKAR[:, 768:1024]
        GM1 = TKAR[:, 1024:1280]
        STRN = TKAR[:, 0:512]
        BAND = TKAR[:, 512:1024]
        PFA = TKAR[:, 1024:1536]
        PFB = TKAR[:, 1536:2048]
        RNK = TKAR[:, 2048:2560]
        SELF_ = TKAR[:, 512:1024]
        GM = Etx(nc.sbuf_tensor("GM", [128, 32], F32))
        MSD0 = Etx(nc.sbuf_tensor("MSD0", [128, 256], F32))
        MSD = [MSD0[:, :], TKAR[:, 1280:1536]]
        RTE = [GL[:, :], MSD[0]]
        DEN = Etx(nc.sbuf_tensor("DEN", [128, 256], F32))
        BNB = DEN[:, :].bitcast(BF16)
        BGR = Etx(nc.sbuf_tensor("BGR", [128, EXP], F32))
        TKA = Etx(nc.sbuf_tensor("TKA", [128, 16], F32))
        TKB = Etx(nc.sbuf_tensor("TKB", [128, 16], F32))
        TKC = Etx(nc.sbuf_tensor("TKC", [128, 16], F32))
        TKD = Etx(nc.sbuf_tensor("TKD", [128, 16], F32))
        MIDT = Etx(nc.sbuf_tensor("MIDT", [128, 16], F32))
        PRD = Etx(nc.sbuf_tensor("PRD", [128, 16], F32))
        PRN = Etx(nc.sbuf_tensor("PRN", [128, 16], F32))
        TT1 = Etx(nc.sbuf_tensor("TT1", [128, 16], F32))
        TT2 = Etx(nc.sbuf_tensor("TT2", [128, 16], F32))
        CK4 = Etx(nc.sbuf_tensor("CK4", [128, 64], F32))
        CMPB = TKAR[:, 0:256].bitcast(BF16).rearrange(
            "p (g t) -> p g t", g=16)
        CNTF = Etx(nc.sbuf_tensor("CNTF", [128, 16], F32))
        CNTP = Etx(nc.sbuf_tensor("CNTP", [128, 16], BF16))
        NEED = Etx(nc.sbuf_tensor("NEED", [128, 16], F32))

        SELB = RNK.bitcast(BF16)[:, 0:512]
        GLB = [GL[:, :], TKAR[:, 1536:1792]]

        TRI = Etx(nc.sbuf_tensor("TRI", [128, 128], BF16))
        ONQ = Etx(nc.sbuf_tensor("ONQ", [128, 128], BF16))
        E16 = Etx(nc.sbuf_tensor("E16", [16, 8, 128], BF16))
        IDN = Etx(nc.sbuf_tensor("IDN", [128, 128], BF16))
        ONE1 = Etx(nc.sbuf_tensor("ONE1", [1, 128], BF16))
        BQR = WA[0:1, 28672:29696]
        ONE5 = WA[0:1, 30720:31232]
        BQF = Etx(nc.sbuf_tensor("BQF", [128, NC], F32))
        B1F = Etx(nc.sbuf_tensor("B1F", [128, 2, EXP], F32))
        BRW = Etx(nc.sbuf_tensor("BRW", [EXP, 2048], BF16))
        BKR = BRW[0:1, 0:1024]
        BVR = BRW[0:1, 1024:2048]
        BOR = BRW[0:1, 0:1024]
        B2B = BRW[0:8, 1024:2048]

        WGB = Etx(nc.sbuf_tensor("WGB", [128, NC, EXP], BF16))

        PS = Etx(nc.psum_tensor("PS", [128, 4096], F32))
        PSbf = PS[:, :].bitcast(BF16)

        def bank(i, r=128):
            return PS[0:r, i * 512:(i + 1) * 512]

        p.sem(nc, stack,
              "sx0", "sx1", "sx2", "sx3",
              "sn0", "sn1", "sq0", "sq1",
              "sr0", "sr1", "sm0", "sm1",
              "so0", "so1",
              "sw", "ssp", "scc", "scd",
              "tp", "vp", "sp")

        blk = Etx(nc.Block())

        tph, vph, sph, ssph = {}, {}, {}, {}

        p.dma("gpsimd", TRI[:, :], tri_d, "sw")
        p.dma("gpsimd", ONQ[:, :], onq_d, "sw")
        p.dma("gpsimd", E16[:, :, :], e16_d, "sw")
        p.dma("gpsimd", IDN[:, :], id_d, "sw")
        p.dma("gpsimd", BQF[:, :], bqf_d, "sw")
        p.dma("gpsimd", B1F[:, :, :], b1f_d, "sw")
        p.dma("gpsimd", BKR, bk_d, "sw")
        p.dma("gpsimd", BQR, bqr_d, "sw")
        p.dma("gpsimd", BVR, bv_d, "sw")
        p.dma("gpsimd", WGB[:, :, :], wg_d, "sw")
        p.dma("gpsimd", BGR[:, :], bg_d.broadcast_to([128, EXP]), "sw")
        p.dma("gpsimd", HMC[:, :, :], hm_d, "sw")
        p.dma("gpsimd", CK4[:, :], ck4_d.broadcast_to([128, 64]), "sw")
        w_ck = p.at("sw")
        p.dma("gpsimd", wq_b, wq_d, "sw")
        p.dma("gpsimd", wk_b, wk_d, "sw")
        p.dma("gpsimd", wv_b, wv_d, "sw")
        w_pre = p.at("sw")
        p.add("vector", lambda e: e.memset(ONE1[:, :], 1.0), inc="vp")
        p.add("vector", lambda e: e.memset(ONE5[:, :], 1.0), inc="vp")
        for i in range(2):
            p.add("vector", lambda e, i=i: e.memset(VSB[i][:, :, 64:65], 1.0),
                  inc="vp")
        v_init = p.at("vp")

        def plan_A(b, ngroups=NG, post=True, stage=9):
            def ln_stats_tile(g2, a):
                t = g2 * NTG + a
                ti = b * NT + t
                w = []
                if ti >= 4:
                    w = [sph[("xlfree", ti - 4)], vph[("xlsq", ti - 4)]]
                p.dma("sync", XL[ti % 4],
                      x[b, t * 128:(t + 1) * 128, :], f"sx{ti % 4}", waits=w)
                xt = XL[ti % 4]
                wld = [(f"sx{ti % 4}", 16 * (ti // 4 + 1))]
                p.add("vector", lambda e, xt=xt, a=a: e.tensor_reduce(
                    ST4[:, a:a + 1], xt, AX, OP.add), waits=wld)
                p.add("vector", lambda e, xt=xt: e.tensor_mul(SQS, xt, xt))
                p.add("vector", lambda e, a=a: e.tensor_reduce(
                    ST4[:, 4 + a:5 + a], SQS, AX, OP.add), inc="vp")
                vph[("xlsq", ti)] = p.at("vp")

            def ln_normalize(g2, a):
                t = g2 * NTG + a
                ti = b * NT + t
                xt = XL[ti % 4]
                xm = XTM[ti % 2]
                p.add("scalar", lambda e, xt=xt, xm=xm, a=a: e.activation(
                    xm, xt, AF.Identity,
                    bias=STB[:, a:a + 1], scale=STA[:, a:a + 1]),
                    waits=[vph[("nrm", (b, g2))], tph.get(("xtm", ti - 2))],
                    inc="sp")
                sph[("xlfree", ti)] = p.at("sp")
                sph[("nrmt", ti)] = p.at("sp")

            def ln_finish(g2):
                p.add("vector", lambda e: e.tensor_scalar_mul(
                    ST4[:, 8:12], ST4[:, 0:4], 1.0 / D))
                p.add("vector", lambda e: e.tensor_scalar_mul(
                    ST4[:, 12:16], ST4[:, 4:8], 1.0 / D))
                p.add("vector", lambda e: e.tensor_mul(
                    STA[:, :], ST4[:, 8:12], ST4[:, 8:12]))
                p.add("vector", lambda e: e.tensor_sub(
                    STA[:, :], ST4[:, 12:16], STA[:, :]))
                p.add("vector", lambda e: e.tensor_scalar_add(
                    STA[:, :], STA[:, :], 1e-5), inc="vp")
                v_var = p.at("vp")
                p.add("scalar", lambda e: e.activation(
                    STB[:, :], STA[:, :], AF.Ln), waits=[v_var])
                p.add("scalar", lambda e: e.activation(
                    STA[:, :], STB[:, :], AF.Exp, scale=-0.5), inc="sp")
                s_rstd = p.at("sp")
                p.add("vector", lambda e: e.scalar_tensor_tensor(
                    STB[:, :], ST4[:, 8:12], -1.0, STA[:, :], OP.mult, OP.mult),
                    waits=[s_rstd], inc="vp")
                vph[("nrm", (b, g2))] = p.at("vp")
                ln_normalize(g2, 0)
                ln_normalize(g2, 1)

            for a in range(NTG):
                ln_stats_tile(0, a)
            ln_finish(0)

            for g in range(ngroups):
                gg = b * NG + g
                buf = gg % 2
                xnf = XNF[buf]
                for a in range(NTG):
                    t = g * NTG + a
                    ti = b * NT + t
                    xm = XTM[ti % 2]
                    lastt = b * NT + g * NTG - 1
                    wtr = [sph[("nrmt", ti)]]
                    wtr4 = []
                    if a == 0:
                        wtr += [vph.get(("kpsfree", lastt)),
                                sph.get(("kpsfree", lastt))]
                        wtr4 += [vph.get(("vpsfree", lastt))]
                        if gg == 0:
                            wtr += [w_pre, v_init]
                    for c in range(NC):
                        colo = c * 512 + a * 128
                        dst = PSbf[:, colo: colo + 128]
                        w_c = wtr if c == 0 else (wtr4 if c == 4 else [])
                        p.add("tensor", lambda e, xm=xm, c=c, dst=dst: e.matmul(
                            dst, xm[:, c * 128:(c + 1) * 128], IDN[:, :],
                            start=True, stop=True, is_transpose=True),
                            waits=w_c,
                            inc="tp" if c == NC - 1 else None)
                    tph[("xtm", ti)] = p.at("tp")
                    if a < 2 and a + 2 < NTG:
                        ln_normalize(g, a + 2)
                t_tr = p.at("tp")

                wev = [t_tr, tph.get(("xnf_used", gg - 2)),
                       ssph.get(("xnsp", gg - 2))]
                p.add("vector", lambda e, xnf=xnf: e.tensor_copy(
                    xnf[:, :, :], PSbf[:, 0:4096]), waits=wev, inc="vp")
                v_xnf = p.at("vp")
                p.dma("gpsimd",
                      xn_sp[b, g, :, :, :],
                      xnf[:, :, :], "ssp", waits=[v_xnf])
                ssph[("xnsp", gg)] = p.at("ssp")

                for a in range(NTG):
                    for c in range(NC):
                        dstg = PS[:, 4 * 512 + a * 8: 4 * 512 + a * 8 + 8]
                        w0 = []
                        if a == 0 and c == 0:
                            w0 = [v_xnf, vph.get(("qpsfree", gg - 1)),
                                  sph.get(("qpsfree", gg - 1))]
                        p.add("tensor", lambda e, dstg=dstg, c=c, a=a, xnf=xnf: e.matmul(
                            dstg, xnf[:, c, a * 128:(a + 1) * 128], WGB[:, c, :],
                            start=(c == 0), stop=(c == NC - 1)),
                            waits=w0,
                            inc="tp" if (a == NTG - 1 and c == NC - 1) else None)
                t_gate = p.at("tp")
                p.add("vector", lambda e, g=g, b=b: e.tensor_copy(
                    GLB[b][:, g * 32:(g + 1) * 32], PS[:, 4 * 512:4 * 512 + 32]),
                    waits=[t_gate], inc="vp")
                v_gate = p.at("vp")

                preq = (p.at("vp"), p.at("sp"))
                q_ev = {-2: preq, -1: preq}
                pboff = {0: (0, 2048), 1: (512, 1024)}
                scrv = SCR[:, 0:1024].rearrange("p (i c) -> p i c", c=512)
                scmv = SCM[:, 0:1024].rearrange("p (i c) -> p i c", c=512)
                scev = SCE[:, 0:1024].rearrange("p (i c) -> p i c", c=512)
                for dq2 in range(4):
                    d0 = 2 * dq2
                    offs = pboff[dq2 % 2]
                    if dq2 % 2 == 0:
                        pvv = PS[:, 0:4096].rearrange(
                            "p (i c) -> p i c", c=2048)[:, :, 0:512]
                    else:
                        pvv = PS[:, 512:1536].rearrange(
                            "p (i c) -> p i c", c=512)
                    w0 = list(q_ev[dq2 - 2])
                    for j in range(2):
                        dqc = d0 + j
                        dst = PS[:, offs[j]:offs[j] + 512]
                        for c in range(NC):
                            p.add("tensor", lambda e, dst=dst, c=c, dqc=dqc, xnf=xnf: e.matmul(
                                dst, wq_b[:, c, dqc * 128:(dqc + 1) * 128],
                                xnf[:, c, :],
                                start=(c == 0), stop=False),
                                waits=w0 if (j == 0 and c == 0) else [])
                        p.add("tensor", lambda e, dst=dst, dqc=dqc: e.matmul(
                            dst, BQR[0:1, dqc * 128:(dqc + 1) * 128],
                            ONE5[0:1, :], start=False, stop=True),
                            inc="tp" if j == 1 else None)
                    t_q = p.at("tp")
                    guard = vph.get(("eluaddq", (gg, dq2 - 1)),
                                    vph.get(("eluadd", b * NT + g * NTG - 1)))
                    p.add("scalar", lambda e, pvv=pvv: e.activation(
                        scrv, pvv, AF.Relu), waits=[t_q, guard], inc="sp")
                    s_relu = p.at("sp")
                    p.add("vector", lambda e, pvv=pvv: e.scalar_tensor_tensor(
                        scmv, scrv, -1.0, pvv, OP.mult, OP.add),
                        waits=[s_relu, guard], inc="vp")
                    q_ev[dq2] = (p.at("vp"), s_relu)
                    p.add("scalar", lambda e: e.activation(
                        scev, scmv, AF.Exp), waits=[p.at("vp")], inc="sp")
                    s_exp = p.at("sp")
                    wqs = [s_exp, s_relu]
                    if dq2 == 0:
                        wqs.append(ssph.get(("qsp", gg - 1)))
                    p.add("vector", lambda e, d0=d0: e.tensor_tensor(
                        QSB[:, d0:d0 + 2, :], scev, scrv, OP.add),
                        waits=wqs, inc="vp")
                    vph[("eluaddq", (gg, dq2))] = p.at("vp")
                vph[("eluaddq", (gg, NC - 1))] = p.at("vp")
                vph[("qpsfree", gg)] = p.at("vp")
                sph[("qpsfree", gg)] = p.at("sp")
                vph[("qb0free", gg)] = q_ev[3][0]
                sph[("qb0free", gg)] = q_ev[3][1]
                q_done = p.at("vp")
                p.dma("gpsimd",
                      q_sp[b, g, :, :, :],
                      QSB[:, :, :], "ssp", waits=[q_done])
                ssph[("qsp", gg)] = p.at("ssp")

                for a in range(NTG):
                    t = g * NTG + a
                    ti = b * NT + t
                    kb = KSB[ti % 2]
                    vb = VSB[ti % 2]
                    w0 = [vph.get(("kpsfree", ti - 1)), sph.get(("kpsfree", ti - 1)),
                          tph.get(("kvmm", ti - 2))]
                    if a == 0:
                        w0 += [vph.get(("qb0free", gg)), sph.get(("qb0free", gg))]
                    for c in range(NC):
                        for jh in range(2):
                            p.add("tensor", lambda e, c=c, jh=jh, a=a, xnf=xnf: e.matmul(
                                bank(jh), xnf[:, c, a * 128:(a + 1) * 128],
                                wk_b[:, c, jh * 512:(jh + 1) * 512],
                                start=(c == 0), stop=False),
                                waits=w0 if (c == 0 and jh == 0) else [])
                    for jh in range(2):
                        p.add("tensor", lambda e, jh=jh: e.matmul(
                            bank(jh), ONE1[:, :], BKR[:, jh * 512:(jh + 1) * 512],
                            start=False, stop=True),
                            inc="tp" if jh == 1 else None)
                    t_k = p.at("tp")
                    w0v = [vph.get(("vpsfree", ti - 1))]
                    for c in range(NC):
                        for jh in range(2):
                            p.add("tensor", lambda e, c=c, jh=jh, a=a, xnf=xnf: e.matmul(
                                bank(2 + jh), xnf[:, c, a * 128:(a + 1) * 128],
                                wv_b[:, c, jh * 512:(jh + 1) * 512],
                                start=(c == 0), stop=False),
                                waits=w0v if (c == 0 and jh == 0) else [])
                    for jh in range(2):
                        p.add("tensor", lambda e, jh=jh: e.matmul(
                            bank(2 + jh), ONE1[:, :], BVR[:, jh * 512:(jh + 1) * 512],
                            start=False, stop=True),
                            inc="tp" if jh == 1 else None)
                    t_v = p.at("tp")

                    w_el = vph.get(("eluadd", ti - 1)) if a > 0 else \
                        vph.get(("eluaddq", (gg, NC - 1)))
                    p.add("scalar", lambda e, a=a: e.activation(
                        SCR, PS[:, 0:1024], AF.Relu),
                        waits=[t_k, w_el], inc="sp")
                    s_relu = p.at("sp")
                    p.add("vector", lambda e: e.tensor_single_scalar(
                        SCM, PS[:, 0:1024], 0.0, OP.min),
                        waits=[s_relu, w_el], inc="vp")
                    vph[("kpsfree", ti)] = p.at("vp")
                    sph[("kpsfree", ti)] = s_relu
                    p.add("scalar", lambda e: e.activation(
                        SCE, SCM, AF.Exp),
                        waits=[p.at("vp")], inc="sp")
                    s_exp = p.at("sp")
                    p.add("vector", lambda e, kb=kb: e.tensor_add(
                        kb[:, :], SCE, SCR), waits=[s_exp, s_relu], inc="vp")
                    v_k = p.at("vp")
                    vph[("eluadd", ti)] = v_k
                    p.add("vector", lambda e, vb=vb: e.tensor_copy(
                        vb[:, :, 0:64],
                        PS[:, 1024:2048].rearrange("p (h n) -> p h n", h=H)),
                        waits=[t_v] + ([v_init] if ti < 2 else []), inc="vp")
                    v_v = p.at("vp")
                    vph[("vpsfree", ti)] = v_v

                    if g + 1 < ngroups:
                        ln_stats_tile(g + 1, a)

                    for h in range(H):
                        col = KVOFF + kvcol(h)
                        dst = PS[0:64, col:col + 65]
                        w0 = [v_k, v_v] if h == 0 else []
                        if h == 0 and t == 0 and b == 1:
                            w0.append(vph.get(("kvtcopy", 0)))
                        p.add("tensor", lambda e, dst=dst, h=h, kb=kb, vb=vb, t=t: e.matmul(
                            dst, kb[:, h * 64:(h + 1) * 64], vb[:, h, :],
                            start=(t == 0 and h in (0, 7, 14)),
                            stop=(t == NT - 1 and h in (6, 13, 15))),
                            waits=w0, inc="tp" if h == H - 1 else None)
                    tph[("kvmm", ti)] = p.at("tp")

                tph[("xnf_used", gg)] = p.at("tp")
                if g + 1 < ngroups:
                    ln_finish(g + 1)

            if post and ngroups == NG:
                plan_kvevac(b)

        def plan_gatepost(b):
            GLx = GLB[b]
            v3 = lambda t: t[:, :].rearrange("p (t e) -> p t e", e=EXP)
            bgb = BGR[:, :].unsqueeze(1).broadcast_to([128, 32, EXP])
            gmb = GM[:, :].unsqueeze(2).broadcast_to([128, 32, EXP])
            p.add("vector", lambda e: e.tensor_tensor(
                v3(GS), v3(GLx), bgb, OP.add), inc="vp")
            p.add("scalar", lambda e: e.activation(
                GEQ, GS, AF.Exp), waits=[p.at("vp")], inc="sp")
            s_exp = p.at("sp")
            p.add("vector", lambda e: e.tensor_reduce(
                GM[:, :], v3(GEQ), AX, OP.add), waits=[s_exp])
            p.add("vector", lambda e: e.reciprocal(GS[:, 0:32], GM[:, :]))
            p.add("vector", lambda e: e.tensor_copy(GM[:, :], GS[:, 0:32]))
            p.add("vector", lambda e: e.tensor_tensor(
                v3(GLx), v3(GEQ), gmb, OP.mult))
            p.add("vector", lambda e: e.tensor_reduce(
                GM[:, :], v3(GLx), AX, OP.max))
            p.add("vector", lambda e: e.tensor_tensor(
                v3(GEQ), v3(GLx), gmb, OP.is_equal))

            def prefix_excl(src, tmp1, tmp2, dst):
                v = v3
                p.add("vector", lambda e: e.tensor_tensor(
                    v(tmp1)[:, :, 1:8], v(src)[:, :, 1:8], v(src)[:, :, 0:7], OP.add))
                p.add("vector", lambda e: e.tensor_copy(
                    v(tmp1)[:, :, 0:1], v(src)[:, :, 0:1]))
                p.add("vector", lambda e: e.tensor_tensor(
                    v(tmp2)[:, :, 2:8], v(tmp1)[:, :, 2:8], v(tmp1)[:, :, 0:6], OP.add))
                p.add("vector", lambda e: e.tensor_copy(
                    v(tmp2)[:, :, 0:2], v(tmp1)[:, :, 0:2]))
                p.add("vector", lambda e: e.tensor_tensor(
                    v(tmp1)[:, :, 4:8], v(tmp2)[:, :, 4:8], v(tmp2)[:, :, 0:4], OP.add))
                p.add("vector", lambda e: e.tensor_copy(
                    v(tmp1)[:, :, 0:4], v(tmp2)[:, :, 0:4]))
                p.add("vector", lambda e: e.tensor_sub(
                    dst[:, :], tmp1[:, :], src[:, :]))

            prefix_excl(GEQ, GCA, GCB, GCB)
            p.add("vector", lambda e: e.tensor_single_scalar(
                GCA, GCB, 0.5, OP.is_lt))
            p.add("vector", lambda e: e.tensor_mul(GM1, GCA, GEQ))
            p.add("vector", lambda e: e.scalar_tensor_tensor(
                GEQ, GM1, -1e30, GLx[:, :], OP.mult, OP.add))
            p.add("vector", lambda e: e.tensor_reduce(
                GM[:, :], v3(GEQ), AX, OP.max))
            p.add("vector", lambda e: e.tensor_tensor(
                v3(GCA), v3(GEQ), gmb, OP.is_equal))
            prefix_excl(GCA, GCB, GEQ, GS)
            p.add("vector", lambda e: e.tensor_single_scalar(
                GCB, GS, 0.5, OP.is_lt))
            p.add("vector", lambda e: e.tensor_mul(GS, GCB, GCA))
            p.add("vector", lambda e: e.tensor_add(GCB, GM1, GS))
            p.add("vector", lambda e, b=b: e.tensor_mul(
                MSD[b], GLx[:, :], GCB), inc="vp")
            vph[("msd", b)] = p.at("vp")

        def plan_kvevac(b):
            t_last = tph[("kvmm", b * NT + NT - 1)]
            w0 = [t_last]
            if b == 1:
                w0.append(ssph.get(("kvbuild", 0)))
            p.add("vector", lambda e: e.tensor_copy(
                KVT[0:64, 0:455], PS[0:64, KVOFF:KVOFF + 455]),
                waits=w0, inc="vp")
            p.add("vector", lambda e: e.tensor_copy(
                KVT[0:64, 455:910], PS[0:64, KVOFF + 512:KVOFF + 512 + 455]))
            p.add("vector", lambda e: e.tensor_copy(
                KVT[0:64, 910:1040], PS[0:64, KVOFF + 1024:KVOFF + 1024 + 130]))
            p.add("vector", lambda e, b=b: e.memset(KVB[:, b, :, :], 0.0))
            p.add("vector", lambda e: e.tensor_copy(
                KSM[0:64, :],
                KVT[0:64, :].rearrange("p (h n) -> p h n", h=H)[:, :, 64]),
                inc="vp")
            v_e = p.at("vp")
            vph[("kvtcopy", b)] = v_e
            kv4 = KVT[0:64, :].rearrange("p (c w n) -> p c w n", c=NC, w=2)
            p.dma("gpsimd", KVB[0:64, b, :, 0:64], kv4[:, :, 0, 0:64], "ssp",
                  waits=[v_e])
            p.dma("gpsimd", KVB[64:128, b, :, 64:128], kv4[:, :, 1, 0:64], "ssp")
            p.dma("gpsimd", KSM[64:128, :], KSM[0:64, :], "ssp")
            ssph[("kvbuild", b)] = p.at("ssp")
            p.add("vector", lambda e, b=b: e.tensor_tensor(
                KST[:, b, :, :],
                KSM[:, :].unsqueeze(1).broadcast_to([128, NC, H]),
                HMC[:, :, :], OP.mult),
                waits=[("ssp", p.cnt["ssp"])], inc="vp")
            vph[("kvdone", b)] = p.at("vp")
            sph[("kvdone", b)] = p.at("sp")

        def plan_AR():
            p.add("vector", lambda e: e.tensor_add(
                DEN[:, :], MSD[0], MSD[1]),
                waits=[vph[("msd", 0)], vph[("msd", 1)]], inc="vp")
            p.dma("gpsimd", cc_i, DEN[:, :], "scd", waits=[p.at("vp")])
            p.add("gpsimd", lambda e: e.collective_compute(
                "AllReduce", OP.add, replica_groups=[list(range(8))],
                ins=[cc_i.opt()], outs=[cc_o.opt()]),
                waits=[p.at("scd")], inc="scc")
            p.dma("gpsimd", DEN[:, :], cc_o, "scd", waits=[p.at("scc")])
            w_den = p.at("scd")
            p.add("vector", lambda e: e.tensor_scalar_add(
                GS, DEN[:, :], 1e-6), waits=[w_den])
            p.add("vector", lambda e: e.reciprocal(DEN[:, :], GS))
            for b in range(BPC):
                p.add("vector", lambda e, b=b: e.scalar_tensor_tensor(
                    RTE[b], MSD[b], 16.0, DEN[:, :],
                    OP.mult, OP.mult), inc="vp")

        def plan_topk():
            lo, hi, lon, hin = TKA, TKB, TKC, TKD
            p.add("vector", lambda e: e.memset(TKA[:, :], 0.0))
            p.add("vector", lambda e: e.memset(TKB[:, :], 17.0), inc="vp")
            rv = [RTE[b].rearrange("p (t e) -> p e t", e=EXP)
                  for b in range(BPC)]
            CMP4b = TKAR[:, 0:1024].bitcast(BF16)
            cm5 = CMP4b.rearrange("p (b e k t) -> p b e k t",
                                  b=BPC, e=EXP, k=NP, t=32)
            cmr = CMP4b.rearrange("p (c t) -> p c t", t=32)
            CNT4F = TKAR[:, 1024:1088]
            CNT4P = TKAR[:, 1088:1120].bitcast(BF16)[:, 0:64]
            PRD4 = TKAR[:, 1120:1184]
            PRN4 = TKAR[:, 1184:1248]
            HIC = TKAR[:, 1248:1312]
            MID4 = TKAR[:, 1312:1376]
            ck4v = CK4[:, :].rearrange("p (g k) -> p g k", k=NP)
            mid4v = MID4.rearrange("p (g k) -> p g k", k=NP)
            mid5 = MID4.rearrange("p (b e k) -> p b e k", b=BPC, e=EXP)
            locv = CNT4F.rearrange("p (g k) -> p g k", k=NP)
            hicv = HIC.rearrange("p (g k) -> p g k", k=NP)
            for it in range(ITERS4):
                w0 = [w_ck] if it == 0 else []
                p.add("vector", lambda e, lo=lo, hi=hi: e.tensor_sub(
                    TT1[:, :], hi[:, :], lo[:, :]), waits=w0)
                p.add("vector", lambda e: e.tensor_tensor(
                    mid4v, TT1[:, :].unsqueeze(2).broadcast_to([128, 16, NP]),
                    ck4v, OP.mult))
                p.add("vector", lambda e, lo=lo: e.tensor_tensor(
                    mid4v, mid4v,
                    lo[:, :].unsqueeze(2).broadcast_to([128, 16, NP]), OP.add))
                for b in range(BPC):
                    for k in range(NP):
                        mb = mid5[:, b, :, k].unsqueeze(2).broadcast_to(
                            [128, EXP, 32])
                        p.add("vector", lambda e, b=b, mb=mb, k=k: e.tensor_tensor(
                            cm5[:, b, :, k, :], rv[b], mb, OP.is_ge))
                p.add("vector", lambda e: e.tensor_reduce(
                    CNT4F, cmr, AX, OP.add))
                p.add("vector", lambda e: e.tensor_copy(
                    CNT4P, CNT4F), inc="vp")
                v_c = p.at("vp")
                pb = 6 if it % 2 == 0 else 7
                p.add("tensor", lambda e, pb=pb: e.matmul(
                    PS[:, pb * 512:pb * 512 + 64], ONQ[:, :], CNT4P,
                    start=True, stop=True), waits=[v_c], inc="tp")
                t_c = p.at("tp")
                p.add("vector", lambda e, pb=pb: e.tensor_single_scalar(
                    PRD4, PS[:, pb * 512:pb * 512 + 64], KSEL, OP.is_ge),
                    waits=[t_c])
                p.add("vector", lambda e: e.tensor_mul(CNT4F, PRD4, MID4))
                p.add("vector", lambda e: e.tensor_reduce(
                    TT1[:, :], locv, AX, OP.max))
                p.add("vector", lambda e, lo=lo, lon=lon: e.tensor_tensor(
                    lon[:, :], TT1[:, :], lo[:, :], OP.max))
                p.add("vector", lambda e: e.tensor_scalar(
                    PRN4, PRD4, -1.0, 1.0, OP.mult, OP.add))
                p.add("vector", lambda e: e.tensor_mul(HIC, PRN4, MID4))
                p.add("vector", lambda e: e.scalar_tensor_tensor(
                    HIC, PRD4, 17.0, HIC, OP.mult, OP.add))
                p.add("vector", lambda e: e.tensor_reduce(
                    TT2[:, :], hicv, AX, OP.min))
                p.add("vector", lambda e, hi=hi, hin=hin: e.tensor_tensor(
                    hin[:, :], TT2[:, :], hi[:, :], OP.min), inc="vp")
                lo, lon = lon, lo
                hi, hin = hin, hi
            sv = STRN.rearrange("p (b e t) -> p (b e) t", b=BPC, e=EXP)
            bv = BAND.rearrange("p (b e t) -> p (b e) t", b=BPC, e=EXP)
            for b in range(BPC):
                hb = hi[:, b * 8:(b + 1) * 8].unsqueeze(2).broadcast_to([128, EXP, 32])
                lb = lo[:, b * 8:(b + 1) * 8].unsqueeze(2).broadcast_to([128, EXP, 32])
                p.add("vector", lambda e, b=b, hb=hb: e.tensor_tensor(
                    sv[:, b * 8:(b + 1) * 8, :], rv[b], hb, OP.is_ge))
                p.add("vector", lambda e, b=b, lb=lb: e.tensor_tensor(
                    bv[:, b * 8:(b + 1) * 8, :], rv[b], lb, OP.is_ge))
            p.add("vector", lambda e: e.tensor_sub(
                BAND, BAND, STRN))
            p.add("vector", lambda e: e.tensor_copy(BNB, BAND))
            p.add("vector", lambda e: e.tensor_reduce(
                CNTF[:, :], sv, AX, OP.add))
            p.add("vector", lambda e: e.tensor_copy(
                CNTP[:, :], CNTF[:, :]), inc="vp")
            v_g = p.at("vp")
            p.add("tensor", lambda e: e.matmul(
                PS[:, 6 * 512:6 * 512 + 16], ONQ[:, :], CNTP[:, :],
                start=True, stop=True), waits=[v_g], inc="tp")
            t_g = p.at("tp")
            p.add("vector", lambda e: e.tensor_scalar(
                NEED[:, :], PS[:, 6 * 512:6 * 512 + 16], -1.0, KSEL,
                OP.mult, OP.add), waits=[t_g], inc="vp")
            v_need = p.at("vp")
            p.add("tensor", lambda e: e.matmul(
                PS[:, 0:512], ONQ[:, :], BNB,
                start=True, stop=True), waits=[v_need])
            p.add("tensor", lambda e: e.matmul(
                PS[:, 512:1024], TRI[:, :], BNB,
                start=True, stop=True), inc="tp")
            t_pf = p.at("tp")
            p.add("vector", lambda e: e.tensor_copy(
                PFA, PS[:, 0:512]), waits=[t_pf])
            pv = lambda t: t[:, :].rearrange("p (g t) -> p g t", g=16)
            for k in (1, 2, 4, 8, 16):
                src, dst = (PFA, PFB) if k in (1, 4, 16) else (PFB, PFA)
                p.add("vector", lambda e, k=k, src=src, dst=dst: e.tensor_tensor(
                    pv(dst)[:, :, k:32], pv(src)[:, :, k:32],
                    pv(src)[:, :, 0:32 - k], OP.add))
                p.add("vector", lambda e, k=k, src=src, dst=dst: e.tensor_copy(
                    pv(dst)[:, :, 0:k], pv(src)[:, :, 0:k]))
            p.add("vector", lambda e: e.tensor_copy(
                pv(PFA)[:, :, 1:32], pv(PFB)[:, :, 0:31]))
            p.add("vector", lambda e: e.memset(pv(PFA)[:, :, 0:1], 0.0))
            p.add("vector", lambda e: e.tensor_tensor(
                RNK, PFA, PS[:, 512:1024], OP.add))
            ndb = NEED[:, :].unsqueeze(2).broadcast_to([128, 16, 32])
            p.add("vector", lambda e, ndb=ndb: e.tensor_tensor(
                pv(PFB), pv(RNK), ndb, OP.is_lt))
            p.add("vector", lambda e: e.tensor_mul(
                PFB, PFB, BAND))
            p.add("vector", lambda e: e.tensor_add(
                SELF_, STRN, PFB))
            p.add("vector", lambda e: e.tensor_copy(SELB[:, :], SELF_),
                  inc="vp")
            v_sel = p.at("vp")
            for b in range(BPC):
                po = 0 if b == 0 else 32
                for tau in range(32):
                    src = SELB[:, :].rearrange(
                        "p (b e t) -> p b e t", b=BPC, e=EXP)[:, b, :, tau]
                    dst = PSbf[po:po + 8,
                               b * 4096 + tau * 128: b * 4096 + tau * 128 + 128]
                    p.add("tensor", lambda e, src=src, dst=dst: e.matmul(
                        dst, src, IDN[:, :],
                        start=True, stop=True, is_transpose=True),
                        waits=[v_sel] if (b == 0 and tau == 0) else [],
                        inc="tp" if (b == BPC - 1 and tau == 31) else None)
            t_selt = p.at("tp")
            for b in range(BPC):
                po = 0 if b == 0 else 32
                p.add("vector", lambda e, b=b, po=po: e.tensor_copy(
                    SELT[po:po + 8, :],
                    PSbf[po:po + 8, b * 4096:(b + 1) * 4096]),
                    waits=[t_selt] if b == 0 else [], inc="vp")
            v_st = p.at("vp")
            p.dma("gpsimd", sel_dr[0, :, :], SELT[0:8, :], "ssp", waits=[v_st])
            p.dma("gpsimd", sel_dr[1, :, :], SELT[32:40, :], "ssp")
            vph[("seldone",)] = v_st
            ssph[("selsp",)] = p.at("ssp")

        def plan_Cweights():
            w_t = [tph[("xnf_used", 2 * NG - 1)], vph[("kvdone", 1)]]
            p.dma("gpsimd", w1_b, w1_d, "sw", waits=w_t)
            s_w1 = p.at("sw")
            p.dma("gpsimd", wo_b, wo_d, "sw")
            p.dma("gpsimd", BOR, bo_d, "sw")
            p.dma("gpsimd", B2B, b2_d, "sw")
            p.dma("gpsimd", w2_b, w2_d, "sw")
            return s_w1, p.at("sw")

        def plan_C(b, ngroups=NG, hack_b2=False, w_selt1=None):
            w_all = [vph[("kvdone", 1)], ssp_A_end,
                     tph[("xnf_used", 2 * NG - 1)]]
            w_sel = [vph[("seldone",)], ssph[("selsp",)]]
            def plan_reload(g2):
                gg2 = b * NG + g2
                buf2 = gg2 % 2
                wn = [tph.get(("c_xnfq", gg2 - 2))]
                wq = [tph.get(("c_xnfq", gg2 - 2))]
                wmk = [vph.get(("c_mskq", gg2 - 2))]
                if gg2 == 0 or (b == 1 and g2 == 0):
                    wn = wn + w_all
                    wq = wq + w_all
                    wmk = wmk + w_all + w_sel
                p.dma("sync", XNF[buf2][:, :, :],
                      xn_sp[b, g2, :, :, :], f"sn{buf2}", waits=wn)
                p.dma("sync", QSL[buf2][:, :, :],
                      q_sp[b, g2, :, :, :], f"sq{buf2}", waits=wq)

            zv = {}
            def emit_denom(g2):
                gg2 = b * NG + g2
                buf2 = gg2 % 2
                w_q2 = (f"sq{buf2}", 16 * (gg2 // 2 + 1))
                if g2 == 0:
                    w0 = [vph.get(("c_z", gg2 - 1)), sph.get(("c_h", gg2 - 1)),
                          vph.get(("c_msk", gg2 - 1)), w_q2]
                    if gg2 == 0:
                        w0.append(p.at("vp"))
                else:
                    w0 = [vph[("c_att", (gg2 - 1, 6))], w_q2]
                for c in range(NC):
                    p.add("tensor", lambda e, c=c, b=b, buf2=buf2: e.matmul(
                        PS[0:16, 0:512], KST[:, b, c, :], QSL[buf2][:, c, :],
                        start=(c == 0), stop=(c == NC - 1)),
                        waits=w0 if c == 0 else [],
                        inc="tp" if c == NC - 1 else None)
                t_d = p.at("tp")
                p.add("vector", lambda e: e.tensor_scalar_add(
                    Z2[:, :], PS[0:16, 0:512], 1e-6), waits=[t_d])
                def _recip(e):
                    with nc.allow_low_precision(reason="Z to bf16 matmul operand"):
                        return e.reciprocal(ZBF[:, :], Z2[:, :])
                p.add("vector", _recip, inc="vp")
                vph[("c_z", gg2)] = p.at("vp")
                zv[g2] = p.at("vp")

            plan_reload(0)
            if ngroups > 1:
                plan_reload(1)
            emit_denom(0)
            for g in range(ngroups):
                gg = b * NG + g
                buf = gg % 2
                wmk = [vph.get(("c_msk", gg - 2))]
                if gg == 0 or (b == 1 and g == 0):
                    wmk = wmk + w_all + w_sel
                p.dma("scalar", MSK[buf][:, :, :],
                      sel_dr[b, :, g * 512:(g + 1) * 512].unsqueeze(0)
                      .broadcast_to([128, EXP, 512]), f"sm{buf}", waits=wmk)
                w_qrdy = (f"sq{buf}", 16 * (gg // 2 + 1))
                w_nrdy = (f"sn{buf}", 16 * (gg // 2 + 1))
                w_mrdy = (f"sm{buf}", 16 * (gg // 2 + 1))

                v_z = zv[g]

                base_ti = b * NT + g * NTG
                prev_par_tile = {0: None, 1: None}
                for tprev in (base_ti - 2, base_ti - 1):
                    if tprev >= 0:
                        prev_par_tile[tprev % 2] = tprev
                hfree = {}
                last_mask = None
                att_done = {}

                def emit_att(c):
                    ab = 1 if c % 2 == 0 else 2
                    zb = 0 if c % 2 == 0 else 3
                    w0 = [att_done.get(c - 2), v_z]
                    p.add("tensor", lambda e, ab=ab, c=c, b=b, buf=buf: e.matmul(
                        bank(ab), KVB[:, b, c, :], QSL[buf][:, c, :],
                        start=True, stop=True), waits=w0)
                    p.add("tensor", lambda e, zb=zb, c=c: e.matmul(
                        bank(zb), E16[:, c, :], ZBF[:, :],
                        start=True, stop=True), inc="tp")
                    t_a = p.at("tp")
                    zr = ZRS[c % 2]
                    p.add("vector", lambda e, zb=zb, zr=zr: e.tensor_copy(
                        zr, bank(zb)), waits=[t_a])
                    p.add("vector", lambda e, ab=ab, c=c, zr=zr: e.tensor_tensor(
                        ATT[:, c, :], bank(ab), zr, OP.mult), inc="vp")
                    att_done[c] = p.at("vp")
                    vph[("c_att", (gg, c))] = p.at("vp")

                def emit_w1(ei):
                    nonlocal last_mask
                    hb = (4, 5) if ei % 2 == 0 else (6, 7)
                    par = 0 if ei % 2 == 0 else 1
                    w0 = list(hfree.get(ei - 2, ()))
                    w0.append(w_nrdy)
                    if gg == 0 and ei in (0, 1):
                        w0.append(sw_w1)
                    if ei in (0, 1) and prev_par_tile[par] is not None:
                        w0.append(vph.get(("c_out", prev_par_tile[par])))
                    for hc in range(2):
                        for c in range(NC):
                            p.add("tensor", lambda e, hb=hb, hc=hc, c=c, ei=ei, buf=buf: e.matmul(
                                bank(hb[hc]), w1_b[:, c, ei, hc * 128:(hc + 1) * 128],
                                XNF[buf][:, c, :],
                                start=(c == 0), stop=(c == NC - 1)),
                                waits=w0 if (hc == 0 and c == 0) else [],
                                inc="tp" if (hc == 1 and c == NC - 1) else None)
                    t_h = p.at("tp")
                    for hc in range(2):
                        wg0 = [t_h] if hc == 0 else []
                        if last_mask is not None:
                            wg0 = wg0 + [last_mask]
                        p.add("scalar", lambda e, hb=hb, hc=hc, ei=ei: e.activation(
                            SCG, bank(hb[hc]), AF.Gelu,
                            bias=B1F[:, hc, ei:ei + 1]),
                            waits=wg0, inc="sp")
                        s_g = p.at("sp")
                        wm0 = [s_g]
                        if ei == 0 and hc == 0:
                            wm0 += [w_mrdy, tph.get(("c_xnf", gg - 1))]
                        p.add("vector", lambda e, ei=ei, hc=hc, buf=buf: e.tensor_tensor(
                            HSB[:, ei, hc, :], SCG, MSK[buf][:, ei, :],
                            OP.mult), waits=wm0, inc="vp")
                        last_mask = p.at("vp")
                    hfree[ei] = (p.at("vp"), p.at("sp"))

                emit_att(0)
                emit_att(1)
                for c in range(2, NC):
                    emit_att(c)
                    emit_w1(c - 2)
                emit_w1(6)
                emit_w1(7)
                v_att = p.at("vp")
                v_h = p.at("vp")
                sph[("c_h", gg)] = p.at("sp")
                vph[("c_msk", gg)] = v_h
                vph[("c_mskq", gg)] = v_h
                tph[("c_xnfq", gg)] = p.at("tp")
                if g + 1 < ngroups:
                    emit_denom(g + 1)
                if g + 2 < ngroups:
                    plan_reload(g + 2)

                for a in range(NTG):
                    tau = g * NTG + a
                    ti = b * NT + tau
                    par = ti % 2
                    fb = (4, 5) if par == 0 else (6, 7)
                    w0 = [vph.get(("c_out", ti - 2)), v_h]
                    if gg == 0 and a == 0:
                        w0.append(sw_all)
                    first = True
                    for c in range(NC):
                        for jh in range(2):
                            p.add("tensor", lambda e, fb=fb, c=c, jh=jh, a=a: e.matmul(
                                bank(fb[jh]), ATT[:, c, a * 128:(a + 1) * 128],
                                wo_b[:, c, jh * 512:(jh + 1) * 512],
                                start=(c == 0), stop=False),
                                waits=w0 if first else [])
                            first = False
                    for ei in range(EXP):
                        for hc in range(2):
                            for jh in range(2):
                                p.add("tensor", lambda e, fb=fb, ei=ei, hc=hc, jh=jh, a=a: e.matmul(
                                    bank(fb[jh]),
                                    HSB[:, ei, hc, a * 128:(a + 1) * 128],
                                    w2_b[:, hc, ei, jh * 512:(jh + 1) * 512],
                                    start=False, stop=False))
                    for jh in range(2):
                        wb2 = [w_selt1] if (w_selt1 is not None and g == 0
                                            and a == 0 and jh == 0) else []
                        p.add("tensor", lambda e, fb=fb, jh=jh, tau=tau: e.matmul(
                            bank(fb[jh]),
                            SELT[0:8, tau * 128:(tau + 1) * 128],
                            B2B[:, jh * 512:(jh + 1) * 512],
                            start=False, stop=False), waits=wb2)
                    for jh in range(2):
                        p.add("tensor", lambda e, fb=fb, jh=jh: e.matmul(
                            bank(fb[jh]), ONE1[:, :], BOR[:, jh * 512:(jh + 1) * 512],
                            start=False, stop=True),
                            inc="tp" if jh == 1 else None)
                    t_f = p.at("tp")
                    p.dma("scalar", XRS[par],
                          x[b, tau * 128:(tau + 1) * 128, :], f"sr{par}",
                          waits=[vph.get(("c_out", ti - 2))])
                    wev = [t_f, (f"sr{par}", 16 * (ti // 2 + 1)),
                           (f"so{par}", 16 * (ti // 2))]
                    p.add("vector", lambda e, fb=fb, par=par: e.tensor_tensor(
                        OUS[par],
                        PS[:, fb[0] * 512:(fb[0] + 2) * 512], XRS[par],
                        OP.add), waits=wev, inc="vp")
                    vph[("c_out", ti)] = p.at("vp")
                    p.dma("sync", out[b, tau * 128:(tau + 1) * 128, :],
                          OUS[par], f"so{par}",
                          waits=[vph[("c_out", ti)]])
                tph[("c_xnf", gg)] = p.at("tp")
                tph[("c_q", gg)] = p.at("tp")

        if phases == "pre":
            pass
        elif phases.startswith("s"):
            plan_A(0, ngroups=1, post=False, stage=float(phases[1:]))
        elif phases.startswith("g"):
            spec = phases[1:]
            ngroups = int(spec.rstrip("p"))
            plan_A(0, ngroups=ngroups, post=spec.endswith("p"))
            if spec.endswith("p"):
                plan_gatepost(0)
        else:
            plan_A(0)
            plan_A(1)
            plan_gatepost(0)
            plan_gatepost(1)
        ssp_A_end = ("ssp", p.cnt["ssp"])
        sw_w1 = sw_all = None
        if phases in ("at", "full", "fulld") or phases.startswith(("c", "d")):
            sw_w1, sw_all = plan_Cweights()
            plan_AR()
        if phases in ("atk", "full") and phases != "at":
            pass
        if phases == "at2":
            plan_topk()
        if phases in ("at", "at2") or phases.startswith("c"):
            pass
        if phases.startswith("c"):
            plan_topk()
            plan_C(0, ngroups=int(phases[1:]))
            p.add("sync", lambda e: e.nop(),
                  waits=[p.at("so0"), p.at("so1")])
        if phases.startswith("d"):
            plan_topk()
            plan_C(0)
            p.dma("gpsimd", SELT[0:8, :], sel_dr[1, :, :], "sw",
                  waits=[tph[("c_xnf", NG - 1)]])
            plan_C(1, ngroups=int(phases[1].rstrip("x")), w_selt1=p.at("sw"))
            p.add("sync", lambda e: e.nop(),
                  waits=[p.at("so0"), p.at("so1")])
        if phases in ("full", "fulld"):
            plan_topk()
            plan_C(0)
            p.dma("gpsimd", SELT[0:8, :], sel_dr[1, :, :], "sw",
                  waits=[tph[("c_xnf", NG - 1)]])
            plan_C(1, w_selt1=p.at("sw"))
            p.add("sync", lambda e: e.nop(),
                  waits=[p.at("so0"), p.at("so1")])
            if phases == "fulld":
                dbgs = {"KST": KST[:, :, :, :], "KVB": KVB[:, :, :, :],
                        "SELT": SELT[0:8, :], "ZBF": ZBF[:, :], "Z2": Z2[:, :],
                        "ATT": ATT[:, :, :], "HSB": HSB[:, :, :, :],
                        "QSL0": QSL[0][:, :, :], "XNF0": XNF[0][:, :, :],
                        "MSK0": MSK[0][:, :, :], "ST4": ST4[:, :], "STA": STA[:, :], "STB": STB[:, :], "XTM0": XTM[0][:, :], "SCMd": SCM, "SCRd": SCR}
                for nm, ap in dbgs.items():
                    dt = ap.dtype
                    shp = list(ap.shape)
                    od = nc.dram_tensor("dbg_" + nm, shp, dt,
                                        kind="ExternalOutput").ap()
                    p.dma("gpsimd", od, ap, "ssp",
                          waits=[p.at("so0"), p.at("so1")])
                p.add("gpsimd", lambda e: e.nop(), waits=[p.at("ssp")])
        elif not phases.startswith(("c", "d")):
            p.add("sync", lambda e: e.nop(),
                  waits=[p.at("ssp"), p.at("sw")])
            p.add("gpsimd", lambda e: e.nop(),
                  waits=[p.at("vp"), p.at("sp"), p.at("tp")])

        blk.sync(lambda e: p.emit("sync", e))
        blk.scalar(lambda e: p.emit("scalar", e))
        blk.vector(lambda e: p.emit("vector", e))
        blk.tensor(lambda e: p.emit("tensor", e))
        blk.gpsimd(lambda e: p.emit("gpsimd", e))

    return nc


_NC_CACHE = None


def _get_nc():
    global _NC_CACHE
    if _NC_CACHE is None:
        _NC_CACHE = build_nc()
    return _NC_CACHE


def make_in_maps(inputs):
    xf = np.ascontiguousarray(inputs["L0_attn_x"], dtype=np.float32)
    Wq = np.asarray(inputs["L0_attn_Wq"], dtype=np.float32)
    Wk = np.asarray(inputs["L0_attn_Wk"], dtype=np.float32)
    Wv = np.asarray(inputs["L0_attn_Wv"], dtype=np.float32)
    Wo = np.asarray(inputs["L0_attn_Wo"], dtype=np.float32)
    bq = np.asarray(inputs["L0_attn_bq"], dtype=np.float32)
    bk = np.asarray(inputs["L0_attn_bk"], dtype=np.float32)
    bv = np.asarray(inputs["L0_attn_bv"], dtype=np.float32)
    bo = np.asarray(inputs["L0_attn_bo"], dtype=np.float32)
    wg = np.asarray(inputs["L0_ffn_w_gate_w"], dtype=np.float32)
    bg = np.asarray(inputs["L0_ffn_w_gate_b"], dtype=np.float32)
    W1 = np.asarray(inputs["L0_ffn_W1"], dtype=np.float32)
    b1 = np.asarray(inputs["L0_ffn_b1"], dtype=np.float32)
    W2 = np.asarray(inputs["L0_ffn_W2"], dtype=np.float32)
    b2 = np.asarray(inputs["L0_ffn_b2"], dtype=np.float32)

    lay = {
        "wq_d": np.ascontiguousarray(Wq.T.reshape(NC, 128, D).transpose(1, 0, 2)),
        "wk_d": np.ascontiguousarray(Wk.T.reshape(NC, 128, D).transpose(1, 0, 2)),
        "wv_d": np.ascontiguousarray(Wv.T.reshape(NC, 128, D).transpose(1, 0, 2)),
        "wo_d": np.ascontiguousarray(Wo.T.reshape(NC, 128, D).transpose(1, 0, 2)),
        "w1_d": np.ascontiguousarray(
            W1.reshape(EXP, NC, 128, 256).transpose(2, 1, 0, 3)),
        "w2_d": np.ascontiguousarray(
            W2.reshape(EXP, 2, 128, D).transpose(2, 1, 0, 3)),
        "bqf_d": np.ascontiguousarray(bq.reshape(NC, 128).T),
        "bqr_d": bq.reshape(1, D),
        "b1f_d": np.ascontiguousarray(b1.reshape(EXP, 2, 128).transpose(2, 1, 0)),
        "bk_d": bk.reshape(1, D),
        "bv_d": bv.reshape(1, D),
        "bo_d": bo.reshape(1, D),
        "b2_d": np.ascontiguousarray(b2),
        "wg_d": np.ascontiguousarray(wg.reshape(NC, 128, EXP).transpose(1, 0, 2)),
        "bg_d": bg.reshape(1, EXP),
        "tri_d": np.triu(np.ones((128, 128), dtype=np.float32), 1),
        "id_d": np.eye(128, dtype=np.float32),
        "onq_d": np.ones((128, 128), dtype=np.float32),
        "ck4_d": np.tile(np.arange(1, 5, dtype=np.float32) / 5.0,
                         16).reshape(1, 64),
    }
    e16 = np.zeros((16, 8, 128), dtype=np.float32)
    for c in range(NC):
        e16[2 * c, c, 0:64] = 1.0
        e16[2 * c + 1, c, 64:128] = 1.0
    lay["e16_d"] = e16
    hm = np.zeros((128, NC, H), dtype=np.float32)
    for c in range(NC):
        hm[0:64, c, 2 * c] = 1.0
        hm[64:128, c, 2 * c + 1] = 1.0
    lay["hm_d"] = hm

    in_maps = []
    for i in range(NCORES):
        m = dict(lay)
        m["x"] = np.ascontiguousarray(xf[i * BPC:(i + 1) * BPC])
        in_maps.append(m)
    return in_maps


def kernel(**inputs):
    in_maps = make_in_maps(inputs)
    nc = _get_nc()
    res = run_bass_kernel_spmd(nc, in_maps, core_ids=list(range(NCORES)))
    return np.concatenate([np.asarray(res.results[i]["out"])
                           for i in range(NCORES)], axis=0)


# version 20 (final)
# speedup vs baseline: 1.0078x; 1.0078x over previous
import contextlib

import numpy as np

import concourse.bass as bass
import concourse.mybir as mybir
from concourse.bass_utils import run_bass_kernel_spmd

F32 = mybir.dt.float32
BF16 = mybir.dt.bfloat16
AX = mybir.AxisListType.X
OP = mybir.AluOpType
AF = mybir.ActivationFunctionType

NCORES = 8
BPC = 2
S = 4096
D = 1024
H = 16
EXP = 8
KSEL = 2048.0
NG = 8
NTG = 4
NT = 32
NC = 8
ITERS = 46
NP = 4
ITERS4 = 20
KVOFF = 5 * 512


def kvcol(h):
    return (h // 7) * 512 + (h % 7) * 65


class Prog:
    def __init__(self):
        self.ops = []
        self.cnt = {}
        self.h = {}

    def sem(self, nc, stack, *names):
        for n in names:
            self.h[n] = stack.enter_context(nc.semaphore(n))
            self.cnt[n] = 0

    def at(self, name):
        return (name, self.cnt[name])

    SELF_SEM = {"vector": "vp", "scalar": "sp"}

    def add(self, eng, fn, waits=(), inc=None, dma=False):
        waits = [w for w in waits if w is not None]
        incs = []
        self_sem = self.SELF_SEM.get(eng)
        if self_sem is not None and not dma:
            waits.append((self_sem, self.cnt[self_sem]))
            if inc is None:
                inc = self_sem
            assert inc == self_sem, (eng, inc)
        if inc is not None:
            delta = 16 if dma else 1
            self.cnt[inc] += delta
            incs.append((inc, delta))
        self.ops.append((eng, fn, waits, incs))

    def dma(self, eng, outap, inap, sem, waits=(), slow=False):
        def fn(e, o=outap, i=inap, s=slow):
            return e.dma_start(out=o, in_=i, allow_slow_non_contiguous=s)
        self.add(eng, fn, waits=waits, inc=sem, dma=True)

    def emit(self, name, eng):
        for (e, fn, waits, incs) in self.ops:
            if e != name:
                continue
            for (s, v) in waits:
                if v > 0:
                    eng.wait_ge(self.h[s], v)
            inst = fn(eng)
            for (s, v) in incs:
                inst.then_inc(self.h[s], v)


def build_nc(phases="full"):
    nc = bass.Bass("TRN2", target_bir_lowering=False, detect_race_conditions=False)

    x = nc.dram_tensor("x", [BPC, S, D], F32, kind="ExternalInput").ap()
    wq_d = nc.dram_tensor("wq_d", [128, NC, D], F32, kind="ExternalInput").ap()
    wk_d = nc.dram_tensor("wk_d", [128, NC, D], F32, kind="ExternalInput").ap()
    wv_d = nc.dram_tensor("wv_d", [128, NC, D], F32, kind="ExternalInput").ap()
    wo_d = nc.dram_tensor("wo_d", [128, NC, D], F32, kind="ExternalInput").ap()
    w1_d = nc.dram_tensor("w1_d", [128, NC, EXP, 256], F32, kind="ExternalInput").ap()
    w2_d = nc.dram_tensor("w2_d", [128, 2, EXP, D], F32, kind="ExternalInput").ap()
    bqf_d = nc.dram_tensor("bqf_d", [128, NC], F32, kind="ExternalInput").ap()
    bqr_d = nc.dram_tensor("bqr_d", [1, D], F32, kind="ExternalInput").ap()
    b1f_d = nc.dram_tensor("b1f_d", [128, 2, EXP], F32, kind="ExternalInput").ap()
    bk_d = nc.dram_tensor("bk_d", [1, D], F32, kind="ExternalInput").ap()
    bv_d = nc.dram_tensor("bv_d", [1, D], F32, kind="ExternalInput").ap()
    bo_d = nc.dram_tensor("bo_d", [1, D], F32, kind="ExternalInput").ap()
    b2_d = nc.dram_tensor("b2_d", [EXP, D], F32, kind="ExternalInput").ap()
    wg_d = nc.dram_tensor("wg_d", [128, NC, EXP], F32, kind="ExternalInput").ap()
    bg_d = nc.dram_tensor("bg_d", [1, EXP], F32, kind="ExternalInput").ap()
    tri_d = nc.dram_tensor("tri_d", [128, 128], F32, kind="ExternalInput").ap()
    e16_d = nc.dram_tensor("e16_d", [16, 8, 128], F32, kind="ExternalInput").ap()
    id_d = nc.dram_tensor("id_d", [128, 128], F32, kind="ExternalInput").ap()
    onq_d = nc.dram_tensor("onq_d", [128, 128], F32, kind="ExternalInput").ap()
    ck4_d = nc.dram_tensor("ck4_d", [1, 64], F32, kind="ExternalInput").ap()
    hm_d = nc.dram_tensor("hm_d", [128, NC, H], F32, kind="ExternalInput").ap()
    out = nc.dram_tensor("out", [BPC, S, D], F32, kind="ExternalOutput").ap()

    xn_sp = nc.dram_tensor("xn_sp", [BPC, NC, 128, S], BF16).ap()
    q_sp = nc.dram_tensor("q_sp", [BPC, NC, 128, S], BF16).ap()
    sel_dr = nc.dram_tensor("sel_dr", [BPC, EXP, S], BF16).ap()
    cc_i = nc.dram_tensor("cc_i", [128, 256], F32).ap()
    cc_o = nc.dram_tensor("cc_o", [128, 256], F32).ap()

    p = Prog()
    with contextlib.ExitStack() as stack:
        Etx = stack.enter_context

        WA = Etx(nc.sbuf_tensor("WA", [128, 40960], BF16))
        wq_b = WA[:, 0:8192].rearrange("p (c n) -> p c n", c=NC)
        wk_b = WA[:, 8192:16384].rearrange("p (c n) -> p c n", c=NC)
        wv_b = WA[:, 16384:24576].rearrange("p (c n) -> p c n", c=NC)
        wo_b = WA[:, 0:8192].rearrange("p (c n) -> p c n", c=NC)
        w1_b = WA[:, 8192:24576].rearrange("p (c e n) -> p c e n", c=NC, e=EXP)
        w2_b = WA[:, 24576:40960].rearrange("p (h e n) -> p h e n", h=2, e=EXP)

        UN = Etx(nc.sbuf_tensor("UN", [128, 6144], F32))
        UNbf = UN[:, :].bitcast(BF16)
        XL = [UN[:, 0:1024], UN[:, 1024:2048], UN[:, 2048:3072], UN[:, 3072:4096]]
        ATT = UNbf[:, 0:4096].rearrange("p (c j) -> p c j", c=NC)
        XRS = [UN[:, 2048:3072], UN[:, 3072:4096]]
        OUS = [UN[:, 4096:5120], UN[:, 5120:6144]]
        XNF = [Etx(nc.sbuf_tensor(f"XNF{i}", [128, NC, 512], BF16)) for i in range(2)]
        XTM = [WA[:, 24576:25600], WA[:, 25600:26624]]
        SCA = Etx(nc.sbuf_tensor("SCA", [128, 4096], F32))
        SCAbf = SCA[:, :].bitcast(BF16)
        SCR = SCAbf[:, 0:1024]
        SCE = SCAbf[:, 1024:2048]
        SCM = SCA[:, 1024:2048]
        SCM = SCA[:, 2048:3072]
        SQS = SCA[:, 1024:2048]
        MSK = [SCAbf[:, 0:4096].rearrange("p (e j) -> p e j", e=EXP),
               SCAbf[:, 4096:8192].rearrange("p (e j) -> p e j", e=EXP)]
        CSC = Etx(nc.sbuf_tensor("CSC", [128, 1024], BF16))
        SCG = CSC[:, 0:512]
        ABUF = Etx(nc.sbuf_tensor("ABUF", [128, 8192], BF16))
        KSB = [ABUF[:, 0:1024], ABUF[:, 1024:2048]]
        VSB = [ABUF[:, 2048:3088].rearrange("p (h j) -> p h j", h=H),
               ABUF[:, 3088:4128].rearrange("p (h j) -> p h j", h=H)]
        HSB = ABUF[:, 0:8192].rearrange("p (e h j) -> p e h j", e=EXP, h=2)
        QSL = [Etx(nc.sbuf_tensor(f"QSL{i}", [128, NC, 512], BF16)) for i in range(2)]
        QSB = QSL[0][:, :, :]
        ZRS = [CSC[:, 0:512], CSC[:, 512:1024]]
        KVB = Etx(nc.sbuf_tensor("KVB", [128, BPC, NC, 128], BF16))
        KST = Etx(nc.sbuf_tensor("KST", [128, BPC, NC, H], BF16))
        KVT = WA[:, 26624:27664]
        KSM = Etx(nc.sbuf_tensor("KSM", [128, H], BF16))
        HMC = Etx(nc.sbuf_tensor("HMC", [128, NC, H], BF16))
        Z2 = Etx(nc.sbuf_tensor("Z2", [16, 512], F32))
        ZBF = Etx(nc.sbuf_tensor("ZBF", [16, 512], BF16))
        SELT = Etx(nc.sbuf_tensor("SELT", [64, S], BF16))

        ST4 = Etx(nc.sbuf_tensor("ST4", [128, 16], F32))
        STA = Etx(nc.sbuf_tensor("STA", [128, 4], F32))
        STB = Etx(nc.sbuf_tensor("STB", [128, 4], F32))
        GL = Etx(nc.sbuf_tensor("GL", [128, 256], F32))
        TKAR = Etx(nc.sbuf_tensor("TKAR", [128, 2560], F32))
        GS = TKAR[:, 0:256]
        GEQ = TKAR[:, 256:512]
        GCA = TKAR[:, 512:768]
        GCB = TKAR[:, 768:1024]
        GM1 = TKAR[:, 1024:1280]
        STRN = TKAR[:, 0:512]
        BAND = TKAR[:, 512:1024]
        PFA = TKAR[:, 1024:1536]
        PFB = TKAR[:, 1536:2048]
        RNK = TKAR[:, 2048:2560]
        SELF_ = TKAR[:, 512:1024]
        GM = Etx(nc.sbuf_tensor("GM", [128, 32], F32))
        MSD0 = Etx(nc.sbuf_tensor("MSD0", [128, 256], F32))
        MSD = [MSD0[:, :], TKAR[:, 1280:1536]]
        RTE = [GL[:, :], MSD[0]]
        DEN = Etx(nc.sbuf_tensor("DEN", [128, 256], F32))
        BNB = DEN[:, :].bitcast(BF16)
        BGR = Etx(nc.sbuf_tensor("BGR", [128, EXP], F32))
        TKA = Etx(nc.sbuf_tensor("TKA", [128, 16], F32))
        TKB = Etx(nc.sbuf_tensor("TKB", [128, 16], F32))
        TKC = Etx(nc.sbuf_tensor("TKC", [128, 16], F32))
        TKD = Etx(nc.sbuf_tensor("TKD", [128, 16], F32))
        MIDT = Etx(nc.sbuf_tensor("MIDT", [128, 16], F32))
        PRD = Etx(nc.sbuf_tensor("PRD", [128, 16], F32))
        PRN = Etx(nc.sbuf_tensor("PRN", [128, 16], F32))
        TT1 = Etx(nc.sbuf_tensor("TT1", [128, 16], F32))
        TT2 = Etx(nc.sbuf_tensor("TT2", [128, 16], F32))
        CK4 = Etx(nc.sbuf_tensor("CK4", [128, 64], F32))
        CMPB = TKAR[:, 0:256].bitcast(BF16).rearrange(
            "p (g t) -> p g t", g=16)
        CNTF = Etx(nc.sbuf_tensor("CNTF", [128, 16], F32))
        CNTP = Etx(nc.sbuf_tensor("CNTP", [128, 16], BF16))
        NEED = Etx(nc.sbuf_tensor("NEED", [128, 16], F32))

        SELB = RNK.bitcast(BF16)[:, 0:512]
        GLB = [GL[:, :], TKAR[:, 1536:1792]]

        TRI = Etx(nc.sbuf_tensor("TRI", [128, 128], BF16))
        ONQ = Etx(nc.sbuf_tensor("ONQ", [128, 128], BF16))
        E16 = Etx(nc.sbuf_tensor("E16", [16, 8, 128], BF16))
        IDN = Etx(nc.sbuf_tensor("IDN", [128, 128], BF16))
        ONE1 = Etx(nc.sbuf_tensor("ONE1", [1, 128], BF16))
        BQR = WA[0:1, 28672:29696]
        ONE5 = WA[0:1, 30720:31232]
        BQF = Etx(nc.sbuf_tensor("BQF", [128, NC], F32))
        B1F = Etx(nc.sbuf_tensor("B1F", [128, 2, EXP], F32))
        BRW = Etx(nc.sbuf_tensor("BRW", [EXP, 2048], BF16))
        BKR = BRW[0:1, 0:1024]
        BVR = BRW[0:1, 1024:2048]
        BOR = BRW[0:1, 0:1024]
        B2B = BRW[0:8, 1024:2048]

        WGB = Etx(nc.sbuf_tensor("WGB", [128, NC, EXP], BF16))

        PS = Etx(nc.psum_tensor("PS", [128, 4096], F32))
        PSbf = PS[:, :].bitcast(BF16)

        def bank(i, r=128):
            return PS[0:r, i * 512:(i + 1) * 512]

        p.sem(nc, stack,
              "sx0", "sx1", "sx2", "sx3",
              "sn0", "sn1", "sq0", "sq1",
              "sr0", "sr1", "sm0", "sm1",
              "so0", "so1",
              "sw", "ssp", "scc", "scd",
              "tp", "vp", "sp")

        blk = Etx(nc.Block())

        tph, vph, sph, ssph = {}, {}, {}, {}

        p.dma("gpsimd", TRI[:, :], tri_d, "sw")
        p.dma("gpsimd", ONQ[:, :], onq_d, "sw")
        p.dma("gpsimd", E16[:, :, :], e16_d, "sw")
        p.dma("gpsimd", IDN[:, :], id_d, "sw")
        p.dma("gpsimd", BQF[:, :], bqf_d, "sw")
        p.dma("gpsimd", B1F[:, :, :], b1f_d, "sw")
        p.dma("gpsimd", BKR, bk_d, "sw")
        p.dma("gpsimd", BQR, bqr_d, "sw")
        p.dma("gpsimd", BVR, bv_d, "sw")
        p.dma("gpsimd", WGB[:, :, :], wg_d, "sw")
        p.dma("gpsimd", BGR[:, :], bg_d.broadcast_to([128, EXP]), "sw")
        p.dma("gpsimd", HMC[:, :, :], hm_d, "sw")
        p.dma("gpsimd", CK4[:, :], ck4_d.broadcast_to([128, 64]), "sw")
        w_ck = p.at("sw")
        p.dma("gpsimd", wq_b, wq_d, "sw")
        p.dma("gpsimd", wk_b, wk_d, "sw")
        p.dma("gpsimd", wv_b, wv_d, "sw")
        w_pre = p.at("sw")
        p.add("vector", lambda e: e.memset(ONE1[:, :], 1.0), inc="vp")
        p.add("vector", lambda e: e.memset(ONE5[:, :], 1.0), inc="vp")
        for i in range(2):
            p.add("vector", lambda e, i=i: e.memset(VSB[i][:, :, 64:65], 1.0),
                  inc="vp")
        v_init = p.at("vp")

        def plan_A(b, ngroups=NG, post=True, stage=9):
            def ln_stats_tile(g2, a):
                t = g2 * NTG + a
                ti = b * NT + t
                w = []
                if ti >= 4:
                    w = [sph[("xlfree", ti - 4)], vph[("xlsq", ti - 4)]]
                p.dma("sync", XL[ti % 4],
                      x[b, t * 128:(t + 1) * 128, :], f"sx{ti % 4}", waits=w)
                xt = XL[ti % 4]
                wld = [(f"sx{ti % 4}", 16 * (ti // 4 + 1))]
                p.add("vector", lambda e, xt=xt, a=a: e.tensor_reduce(
                    ST4[:, a:a + 1], xt, AX, OP.add), waits=wld)
                p.add("vector", lambda e, xt=xt: e.tensor_mul(SQS, xt, xt))
                p.add("vector", lambda e, a=a: e.tensor_reduce(
                    ST4[:, 4 + a:5 + a], SQS, AX, OP.add), inc="vp")
                vph[("xlsq", ti)] = p.at("vp")

            def ln_normalize(g2, a):
                t = g2 * NTG + a
                ti = b * NT + t
                xt = XL[ti % 4]
                xm = XTM[ti % 2]
                p.add("scalar", lambda e, xt=xt, xm=xm, a=a: e.activation(
                    xm, xt, AF.Identity,
                    bias=STB[:, a:a + 1], scale=STA[:, a:a + 1]),
                    waits=[vph[("nrm", (b, g2))], tph.get(("xtm", ti - 2))],
                    inc="sp")
                sph[("xlfree", ti)] = p.at("sp")
                sph[("nrmt", ti)] = p.at("sp")

            def ln_finish(g2):
                p.add("vector", lambda e: e.tensor_scalar_mul(
                    ST4[:, 8:12], ST4[:, 0:4], 1.0 / D))
                p.add("vector", lambda e: e.tensor_scalar_mul(
                    ST4[:, 12:16], ST4[:, 4:8], 1.0 / D))
                p.add("vector", lambda e: e.tensor_mul(
                    STA[:, :], ST4[:, 8:12], ST4[:, 8:12]))
                p.add("vector", lambda e: e.tensor_sub(
                    STA[:, :], ST4[:, 12:16], STA[:, :]))
                p.add("vector", lambda e: e.tensor_scalar_add(
                    STA[:, :], STA[:, :], 1e-5), inc="vp")
                v_var = p.at("vp")
                p.add("scalar", lambda e: e.activation(
                    STB[:, :], STA[:, :], AF.Ln), waits=[v_var])
                p.add("scalar", lambda e: e.activation(
                    STA[:, :], STB[:, :], AF.Exp, scale=-0.5), inc="sp")
                s_rstd = p.at("sp")
                p.add("vector", lambda e: e.scalar_tensor_tensor(
                    STB[:, :], ST4[:, 8:12], -1.0, STA[:, :], OP.mult, OP.mult),
                    waits=[s_rstd], inc="vp")
                vph[("nrm", (b, g2))] = p.at("vp")
                ln_normalize(g2, 0)
                ln_normalize(g2, 1)

            for a in range(NTG):
                ln_stats_tile(0, a)
            ln_finish(0)

            for g in range(ngroups):
                gg = b * NG + g
                buf = gg % 2
                xnf = XNF[buf]
                for a in range(NTG):
                    t = g * NTG + a
                    ti = b * NT + t
                    xm = XTM[ti % 2]
                    lastt = b * NT + g * NTG - 1
                    wtr = [sph[("nrmt", ti)]]
                    wtr4 = []
                    if a == 0:
                        wtr += [vph.get(("kpsfree", lastt)),
                                sph.get(("kpsfree", lastt))]
                        wtr4 += [vph.get(("vpsfree", lastt))]
                        if gg == 0:
                            wtr += [w_pre, v_init]
                    for c in range(NC):
                        colo = c * 512 + a * 128
                        dst = PSbf[:, colo: colo + 128]
                        w_c = wtr if c == 0 else (wtr4 if c == 4 else [])
                        p.add("tensor", lambda e, xm=xm, c=c, dst=dst: e.matmul(
                            dst, xm[:, c * 128:(c + 1) * 128], IDN[:, :],
                            start=True, stop=True, is_transpose=True),
                            waits=w_c,
                            inc="tp" if c == NC - 1 else None)
                    tph[("xtm", ti)] = p.at("tp")
                    if a < 2 and a + 2 < NTG:
                        ln_normalize(g, a + 2)
                t_tr = p.at("tp")

                wev = [t_tr, tph.get(("xnf_used", gg - 2)),
                       ssph.get(("xnsp", gg - 2))]
                p.add("vector", lambda e, xnf=xnf: e.tensor_copy(
                    xnf[:, :, :], PSbf[:, 0:4096]), waits=wev, inc="vp")
                v_xnf = p.at("vp")
                p.dma("gpsimd",
                      xn_sp[b, :, :, g * 512:(g + 1) * 512].transpose([1, 0, 2]),
                      xnf[:, :, :], "ssp", waits=[v_xnf])
                ssph[("xnsp", gg)] = p.at("ssp")

                for a in range(NTG):
                    for c in range(NC):
                        dstg = PS[:, 4 * 512 + a * 8: 4 * 512 + a * 8 + 8]
                        w0 = []
                        if a == 0 and c == 0:
                            w0 = [v_xnf, vph.get(("qpsfree", gg - 1)),
                                  sph.get(("qpsfree", gg - 1))]
                        p.add("tensor", lambda e, dstg=dstg, c=c, a=a, xnf=xnf: e.matmul(
                            dstg, xnf[:, c, a * 128:(a + 1) * 128], WGB[:, c, :],
                            start=(c == 0), stop=(c == NC - 1)),
                            waits=w0,
                            inc="tp" if (a == NTG - 1 and c == NC - 1) else None)
                t_gate = p.at("tp")
                p.add("vector", lambda e, g=g, b=b: e.tensor_copy(
                    GLB[b][:, g * 32:(g + 1) * 32], PS[:, 4 * 512:4 * 512 + 32]),
                    waits=[t_gate], inc="vp")
                v_gate = p.at("vp")

                preq = (p.at("vp"), p.at("sp"))
                q_ev = {-2: preq, -1: preq}
                pboff = {0: (0, 2048), 1: (512, 1024)}
                scrv = SCR[:, 0:1024].rearrange("p (i c) -> p i c", c=512)
                scmv = SCM[:, 0:1024].rearrange("p (i c) -> p i c", c=512)
                scev = SCE[:, 0:1024].rearrange("p (i c) -> p i c", c=512)
                for dq2 in range(4):
                    d0 = 2 * dq2
                    offs = pboff[dq2 % 2]
                    if dq2 % 2 == 0:
                        pvv = PS[:, 0:4096].rearrange(
                            "p (i c) -> p i c", c=2048)[:, :, 0:512]
                    else:
                        pvv = PS[:, 512:1536].rearrange(
                            "p (i c) -> p i c", c=512)
                    w0 = list(q_ev[dq2 - 2])
                    for j in range(2):
                        dqc = d0 + j
                        dst = PS[:, offs[j]:offs[j] + 512]
                        for c in range(NC):
                            p.add("tensor", lambda e, dst=dst, c=c, dqc=dqc, xnf=xnf: e.matmul(
                                dst, wq_b[:, c, dqc * 128:(dqc + 1) * 128],
                                xnf[:, c, :],
                                start=(c == 0), stop=False),
                                waits=w0 if (j == 0 and c == 0) else [])
                        p.add("tensor", lambda e, dst=dst, dqc=dqc: e.matmul(
                            dst, BQR[0:1, dqc * 128:(dqc + 1) * 128],
                            ONE5[0:1, :], start=False, stop=True),
                            inc="tp" if j == 1 else None)
                    t_q = p.at("tp")
                    guard = vph.get(("eluaddq", (gg, dq2 - 1)),
                                    vph.get(("eluadd", b * NT + g * NTG - 1)))
                    p.add("scalar", lambda e, pvv=pvv: e.activation(
                        scrv, pvv, AF.Relu), waits=[t_q, guard], inc="sp")
                    s_relu = p.at("sp")
                    p.add("vector", lambda e, pvv=pvv: e.scalar_tensor_tensor(
                        scmv, scrv, -1.0, pvv, OP.mult, OP.add),
                        waits=[s_relu, guard], inc="vp")
                    q_ev[dq2] = (p.at("vp"), s_relu)
                    p.add("scalar", lambda e: e.activation(
                        scev, scmv, AF.Exp), waits=[p.at("vp")], inc="sp")
                    s_exp = p.at("sp")
                    wqs = [s_exp, s_relu]
                    if dq2 == 0:
                        wqs.append(ssph.get(("qsp", gg - 1)))
                    p.add("vector", lambda e, d0=d0: e.tensor_tensor(
                        QSB[:, d0:d0 + 2, :], scev, scrv, OP.add),
                        waits=wqs, inc="vp")
                    vph[("eluaddq", (gg, dq2))] = p.at("vp")
                vph[("eluaddq", (gg, NC - 1))] = p.at("vp")
                vph[("qpsfree", gg)] = p.at("vp")
                sph[("qpsfree", gg)] = p.at("sp")
                vph[("qb0free", gg)] = q_ev[3][0]
                sph[("qb0free", gg)] = q_ev[3][1]
                q_done = p.at("vp")
                p.dma("gpsimd",
                      q_sp[b, :, :, g * 512:(g + 1) * 512].transpose([1, 0, 2]),
                      QSB[:, :, :], "ssp", waits=[q_done])
                ssph[("qsp", gg)] = p.at("ssp")

                for a in range(NTG):
                    t = g * NTG + a
                    ti = b * NT + t
                    kb = KSB[ti % 2]
                    vb = VSB[ti % 2]
                    w0 = [vph.get(("kpsfree", ti - 1)), sph.get(("kpsfree", ti - 1)),
                          tph.get(("kvmm", ti - 2))]
                    if a == 0:
                        w0 += [vph.get(("qb0free", gg)), sph.get(("qb0free", gg))]
                    for c in range(NC):
                        for jh in range(2):
                            p.add("tensor", lambda e, c=c, jh=jh, a=a, xnf=xnf: e.matmul(
                                bank(jh), xnf[:, c, a * 128:(a + 1) * 128],
                                wk_b[:, c, jh * 512:(jh + 1) * 512],
                                start=(c == 0), stop=False),
                                waits=w0 if (c == 0 and jh == 0) else [])
                    for jh in range(2):
                        p.add("tensor", lambda e, jh=jh: e.matmul(
                            bank(jh), ONE1[:, :], BKR[:, jh * 512:(jh + 1) * 512],
                            start=False, stop=True),
                            inc="tp" if jh == 1 else None)
                    t_k = p.at("tp")
                    w0v = [vph.get(("vpsfree", ti - 1))]
                    for c in range(NC):
                        for jh in range(2):
                            p.add("tensor", lambda e, c=c, jh=jh, a=a, xnf=xnf: e.matmul(
                                bank(2 + jh), xnf[:, c, a * 128:(a + 1) * 128],
                                wv_b[:, c, jh * 512:(jh + 1) * 512],
                                start=(c == 0), stop=False),
                                waits=w0v if (c == 0 and jh == 0) else [])
                    for jh in range(2):
                        p.add("tensor", lambda e, jh=jh: e.matmul(
                            bank(2 + jh), ONE1[:, :], BVR[:, jh * 512:(jh + 1) * 512],
                            start=False, stop=True),
                            inc="tp" if jh == 1 else None)
                    t_v = p.at("tp")

                    w_el = vph.get(("eluadd", ti - 1)) if a > 0 else \
                        vph.get(("eluaddq", (gg, NC - 1)))
                    p.add("scalar", lambda e, a=a: e.activation(
                        SCR, PS[:, 0:1024], AF.Relu),
                        waits=[t_k, w_el], inc="sp")
                    s_relu = p.at("sp")
                    p.add("vector", lambda e: e.tensor_single_scalar(
                        SCM, PS[:, 0:1024], 0.0, OP.min),
                        waits=[s_relu, w_el], inc="vp")
                    vph[("kpsfree", ti)] = p.at("vp")
                    sph[("kpsfree", ti)] = s_relu
                    p.add("scalar", lambda e: e.activation(
                        SCE, SCM, AF.Exp),
                        waits=[p.at("vp")], inc="sp")
                    s_exp = p.at("sp")
                    p.add("vector", lambda e, kb=kb: e.tensor_add(
                        kb[:, :], SCE, SCR), waits=[s_exp, s_relu], inc="vp")
                    v_k = p.at("vp")
                    vph[("eluadd", ti)] = v_k
                    p.add("vector", lambda e, vb=vb: e.tensor_copy(
                        vb[:, :, 0:64],
                        PS[:, 1024:2048].rearrange("p (h n) -> p h n", h=H)),
                        waits=[t_v] + ([v_init] if ti < 2 else []), inc="vp")
                    v_v = p.at("vp")
                    vph[("vpsfree", ti)] = v_v

                    if g + 1 < ngroups:
                        ln_stats_tile(g + 1, a)

                    for h in range(H):
                        col = KVOFF + kvcol(h)
                        dst = PS[0:64, col:col + 65]
                        w0 = [v_k, v_v] if h == 0 else []
                        if h == 0 and t == 0 and b == 1:
                            w0.append(vph.get(("kvtcopy", 0)))
                        p.add("tensor", lambda e, dst=dst, h=h, kb=kb, vb=vb, t=t: e.matmul(
                            dst, kb[:, h * 64:(h + 1) * 64], vb[:, h, :],
                            start=(t == 0 and h in (0, 7, 14)),
                            stop=(t == NT - 1 and h in (6, 13, 15))),
                            waits=w0, inc="tp" if h == H - 1 else None)
                    tph[("kvmm", ti)] = p.at("tp")

                tph[("xnf_used", gg)] = p.at("tp")
                if g + 1 < ngroups:
                    ln_finish(g + 1)

            if post and ngroups == NG:
                plan_kvevac(b)

        def plan_gatepost(b):
            GLx = GLB[b]
            v3 = lambda t: t[:, :].rearrange("p (t e) -> p t e", e=EXP)
            bgb = BGR[:, :].unsqueeze(1).broadcast_to([128, 32, EXP])
            gmb = GM[:, :].unsqueeze(2).broadcast_to([128, 32, EXP])
            p.add("vector", lambda e: e.tensor_tensor(
                v3(GS), v3(GLx), bgb, OP.add), inc="vp")
            p.add("scalar", lambda e: e.activation(
                GEQ, GS, AF.Exp), waits=[p.at("vp")], inc="sp")
            s_exp = p.at("sp")
            p.add("vector", lambda e: e.tensor_reduce(
                GM[:, :], v3(GEQ), AX, OP.add), waits=[s_exp])
            p.add("vector", lambda e: e.reciprocal(GS[:, 0:32], GM[:, :]))
            p.add("vector", lambda e: e.tensor_copy(GM[:, :], GS[:, 0:32]))
            p.add("vector", lambda e: e.tensor_tensor(
                v3(GLx), v3(GEQ), gmb, OP.mult))
            p.add("vector", lambda e: e.tensor_reduce(
                GM[:, :], v3(GLx), AX, OP.max))
            p.add("vector", lambda e: e.tensor_tensor(
                v3(GEQ), v3(GLx), gmb, OP.is_equal))

            def prefix_excl(src, tmp1, tmp2, dst):
                v = v3
                p.add("vector", lambda e: e.tensor_tensor(
                    v(tmp1)[:, :, 1:8], v(src)[:, :, 1:8], v(src)[:, :, 0:7], OP.add))
                p.add("vector", lambda e: e.tensor_copy(
                    v(tmp1)[:, :, 0:1], v(src)[:, :, 0:1]))
                p.add("vector", lambda e: e.tensor_tensor(
                    v(tmp2)[:, :, 2:8], v(tmp1)[:, :, 2:8], v(tmp1)[:, :, 0:6], OP.add))
                p.add("vector", lambda e: e.tensor_copy(
                    v(tmp2)[:, :, 0:2], v(tmp1)[:, :, 0:2]))
                p.add("vector", lambda e: e.tensor_tensor(
                    v(tmp1)[:, :, 4:8], v(tmp2)[:, :, 4:8], v(tmp2)[:, :, 0:4], OP.add))
                p.add("vector", lambda e: e.tensor_copy(
                    v(tmp1)[:, :, 0:4], v(tmp2)[:, :, 0:4]))
                p.add("vector", lambda e: e.tensor_sub(
                    dst[:, :], tmp1[:, :], src[:, :]))

            prefix_excl(GEQ, GCA, GCB, GCB)
            p.add("vector", lambda e: e.tensor_single_scalar(
                GCA, GCB, 0.5, OP.is_lt))
            p.add("vector", lambda e: e.tensor_mul(GM1, GCA, GEQ))
            p.add("vector", lambda e: e.scalar_tensor_tensor(
                GEQ, GM1, -1e30, GLx[:, :], OP.mult, OP.add))
            p.add("vector", lambda e: e.tensor_reduce(
                GM[:, :], v3(GEQ), AX, OP.max))
            p.add("vector", lambda e: e.tensor_tensor(
                v3(GCA), v3(GEQ), gmb, OP.is_equal))
            prefix_excl(GCA, GCB, GEQ, GS)
            p.add("vector", lambda e: e.tensor_single_scalar(
                GCB, GS, 0.5, OP.is_lt))
            p.add("vector", lambda e: e.tensor_mul(GS, GCB, GCA))
            p.add("vector", lambda e: e.tensor_add(GCB, GM1, GS))
            p.add("vector", lambda e, b=b: e.tensor_mul(
                MSD[b], GLx[:, :], GCB), inc="vp")
            vph[("msd", b)] = p.at("vp")

        def plan_kvevac(b):
            t_last = tph[("kvmm", b * NT + NT - 1)]
            w0 = [t_last]
            if b == 1:
                w0.append(ssph.get(("kvbuild", 0)))
            p.add("vector", lambda e: e.tensor_copy(
                KVT[0:64, 0:455], PS[0:64, KVOFF:KVOFF + 455]),
                waits=w0, inc="vp")
            p.add("vector", lambda e: e.tensor_copy(
                KVT[0:64, 455:910], PS[0:64, KVOFF + 512:KVOFF + 512 + 455]))
            p.add("vector", lambda e: e.tensor_copy(
                KVT[0:64, 910:1040], PS[0:64, KVOFF + 1024:KVOFF + 1024 + 130]))
            p.add("vector", lambda e, b=b: e.memset(KVB[:, b, :, :], 0.0))
            p.add("vector", lambda e: e.tensor_copy(
                KSM[0:64, :],
                KVT[0:64, :].rearrange("p (h n) -> p h n", h=H)[:, :, 64]),
                inc="vp")
            v_e = p.at("vp")
            vph[("kvtcopy", b)] = v_e
            kv4 = KVT[0:64, :].rearrange("p (c w n) -> p c w n", c=NC, w=2)
            p.dma("gpsimd", KVB[0:64, b, :, 0:64], kv4[:, :, 0, 0:64], "ssp",
                  waits=[v_e])
            p.dma("gpsimd", KVB[64:128, b, :, 64:128], kv4[:, :, 1, 0:64], "ssp")
            p.dma("gpsimd", KSM[64:128, :], KSM[0:64, :], "ssp")
            ssph[("kvbuild", b)] = p.at("ssp")
            p.add("vector", lambda e, b=b: e.tensor_tensor(
                KST[:, b, :, :],
                KSM[:, :].unsqueeze(1).broadcast_to([128, NC, H]),
                HMC[:, :, :], OP.mult),
                waits=[("ssp", p.cnt["ssp"])], inc="vp")
            vph[("kvdone", b)] = p.at("vp")
            sph[("kvdone", b)] = p.at("sp")

        def plan_AR():
            p.add("vector", lambda e: e.tensor_add(
                DEN[:, :], MSD[0], MSD[1]),
                waits=[vph[("msd", 0)], vph[("msd", 1)]], inc="vp")
            p.dma("gpsimd", cc_i, DEN[:, :], "scd", waits=[p.at("vp")])
            p.add("gpsimd", lambda e: e.collective_compute(
                "AllReduce", OP.add, replica_groups=[list(range(8))],
                ins=[cc_i.opt()], outs=[cc_o.opt()]),
                waits=[p.at("scd")], inc="scc")
            p.dma("gpsimd", DEN[:, :], cc_o, "scd", waits=[p.at("scc")])
            w_den = p.at("scd")
            p.add("vector", lambda e: e.tensor_scalar_add(
                GS, DEN[:, :], 1e-6), waits=[w_den])
            p.add("vector", lambda e: e.reciprocal(DEN[:, :], GS))
            for b in range(BPC):
                p.add("vector", lambda e, b=b: e.scalar_tensor_tensor(
                    RTE[b], MSD[b], 16.0, DEN[:, :],
                    OP.mult, OP.mult), inc="vp")

        def plan_topk():
            lo, hi, lon, hin = TKA, TKB, TKC, TKD
            p.add("vector", lambda e: e.memset(TKA[:, :], 0.0))
            p.add("vector", lambda e: e.memset(TKB[:, :], 17.0), inc="vp")
            rv = [RTE[b].rearrange("p (t e) -> p e t", e=EXP)
                  for b in range(BPC)]
            CMP4b = TKAR[:, 0:1024].bitcast(BF16)
            cm5 = CMP4b.rearrange("p (b e k t) -> p b e k t",
                                  b=BPC, e=EXP, k=NP, t=32)
            cmr = CMP4b.rearrange("p (c t) -> p c t", t=32)
            CNT4F = TKAR[:, 1024:1088]
            CNT4P = TKAR[:, 1088:1120].bitcast(BF16)[:, 0:64]
            PRD4 = TKAR[:, 1120:1184]
            PRN4 = TKAR[:, 1184:1248]
            HIC = TKAR[:, 1248:1312]
            MID4 = TKAR[:, 1312:1376]
            ck4v = CK4[:, :].rearrange("p (g k) -> p g k", k=NP)
            mid4v = MID4.rearrange("p (g k) -> p g k", k=NP)
            mid5 = MID4.rearrange("p (b e k) -> p b e k", b=BPC, e=EXP)
            locv = CNT4F.rearrange("p (g k) -> p g k", k=NP)
            hicv = HIC.rearrange("p (g k) -> p g k", k=NP)
            for it in range(ITERS4):
                w0 = [w_ck] if it == 0 else []
                p.add("vector", lambda e, lo=lo, hi=hi: e.tensor_sub(
                    TT1[:, :], hi[:, :], lo[:, :]), waits=w0)
                p.add("vector", lambda e: e.tensor_tensor(
                    mid4v, TT1[:, :].unsqueeze(2).broadcast_to([128, 16, NP]),
                    ck4v, OP.mult))
                p.add("vector", lambda e, lo=lo: e.tensor_tensor(
                    mid4v, mid4v,
                    lo[:, :].unsqueeze(2).broadcast_to([128, 16, NP]), OP.add))
                for b in range(BPC):
                    for k in range(NP):
                        mb = mid5[:, b, :, k].unsqueeze(2).broadcast_to(
                            [128, EXP, 32])
                        p.add("vector", lambda e, b=b, mb=mb, k=k: e.tensor_tensor(
                            cm5[:, b, :, k, :], rv[b], mb, OP.is_ge))
                p.add("vector", lambda e: e.tensor_reduce(
                    CNT4F, cmr, AX, OP.add))
                p.add("vector", lambda e: e.tensor_copy(
                    CNT4P, CNT4F), inc="vp")
                v_c = p.at("vp")
                pb = 6 if it % 2 == 0 else 7
                p.add("tensor", lambda e, pb=pb: e.matmul(
                    PS[:, pb * 512:pb * 512 + 64], ONQ[:, :], CNT4P,
                    start=True, stop=True), waits=[v_c], inc="tp")
                t_c = p.at("tp")
                p.add("vector", lambda e, pb=pb: e.tensor_single_scalar(
                    PRD4, PS[:, pb * 512:pb * 512 + 64], KSEL, OP.is_ge),
                    waits=[t_c])
                p.add("vector", lambda e: e.tensor_mul(CNT4F, PRD4, MID4))
                p.add("vector", lambda e: e.tensor_reduce(
                    TT1[:, :], locv, AX, OP.max))
                p.add("vector", lambda e, lo=lo, lon=lon: e.tensor_tensor(
                    lon[:, :], TT1[:, :], lo[:, :], OP.max))
                p.add("vector", lambda e: e.scalar_tensor_tensor(
                    HIC, PRD4, 17.0, MID4, OP.mult, OP.add))
                p.add("vector", lambda e: e.tensor_reduce(
                    TT2[:, :], hicv, AX, OP.min))
                p.add("vector", lambda e, hi=hi, hin=hin: e.tensor_tensor(
                    hin[:, :], TT2[:, :], hi[:, :], OP.min), inc="vp")
                lo, lon = lon, lo
                hi, hin = hin, hi
            sv = STRN.rearrange("p (b e t) -> p (b e) t", b=BPC, e=EXP)
            bv = BAND.rearrange("p (b e t) -> p (b e) t", b=BPC, e=EXP)
            for b in range(BPC):
                hb = hi[:, b * 8:(b + 1) * 8].unsqueeze(2).broadcast_to([128, EXP, 32])
                lb = lo[:, b * 8:(b + 1) * 8].unsqueeze(2).broadcast_to([128, EXP, 32])
                p.add("vector", lambda e, b=b, hb=hb: e.tensor_tensor(
                    sv[:, b * 8:(b + 1) * 8, :], rv[b], hb, OP.is_ge))
                p.add("vector", lambda e, b=b, lb=lb: e.tensor_tensor(
                    bv[:, b * 8:(b + 1) * 8, :], rv[b], lb, OP.is_ge))
            p.add("vector", lambda e: e.tensor_sub(
                BAND, BAND, STRN))
            p.add("vector", lambda e: e.tensor_copy(BNB, BAND))
            p.add("vector", lambda e: e.tensor_reduce(
                CNTF[:, :], sv, AX, OP.add))
            p.add("vector", lambda e: e.tensor_copy(
                CNTP[:, :], CNTF[:, :]), inc="vp")
            v_g = p.at("vp")
            p.add("tensor", lambda e: e.matmul(
                PS[:, 6 * 512:6 * 512 + 16], ONQ[:, :], CNTP[:, :],
                start=True, stop=True), waits=[v_g], inc="tp")
            t_g = p.at("tp")
            p.add("vector", lambda e: e.tensor_scalar(
                NEED[:, :], PS[:, 6 * 512:6 * 512 + 16], -1.0, KSEL,
                OP.mult, OP.add), waits=[t_g], inc="vp")
            v_need = p.at("vp")
            p.add("tensor", lambda e: e.matmul(
                PS[:, 0:512], ONQ[:, :], BNB,
                start=True, stop=True), waits=[v_need])
            p.add("tensor", lambda e: e.matmul(
                PS[:, 512:1024], TRI[:, :], BNB,
                start=True, stop=True), inc="tp")
            t_pf = p.at("tp")
            p.add("vector", lambda e: e.tensor_copy(
                PFA, PS[:, 0:512]), waits=[t_pf])
            pv = lambda t: t[:, :].rearrange("p (g t) -> p g t", g=16)
            for k in (1, 2, 4, 8, 16):
                src, dst = (PFA, PFB) if k in (1, 4, 16) else (PFB, PFA)
                p.add("vector", lambda e, k=k, src=src, dst=dst: e.tensor_tensor(
                    pv(dst)[:, :, k:32], pv(src)[:, :, k:32],
                    pv(src)[:, :, 0:32 - k], OP.add))
                p.add("vector", lambda e, k=k, src=src, dst=dst: e.tensor_copy(
                    pv(dst)[:, :, 0:k], pv(src)[:, :, 0:k]))
            p.add("vector", lambda e: e.tensor_copy(
                pv(PFA)[:, :, 1:32], pv(PFB)[:, :, 0:31]))
            p.add("vector", lambda e: e.memset(pv(PFA)[:, :, 0:1], 0.0))
            p.add("vector", lambda e: e.tensor_tensor(
                RNK, PFA, PS[:, 512:1024], OP.add))
            ndb = NEED[:, :].unsqueeze(2).broadcast_to([128, 16, 32])
            p.add("vector", lambda e, ndb=ndb: e.tensor_tensor(
                pv(PFB), pv(RNK), ndb, OP.is_lt))
            p.add("vector", lambda e: e.tensor_mul(
                PFB, PFB, BAND))
            p.add("vector", lambda e: e.tensor_add(
                SELF_, STRN, PFB))
            p.add("vector", lambda e: e.tensor_copy(SELB[:, :], SELF_),
                  inc="vp")
            v_sel = p.at("vp")
            for b in range(BPC):
                po = 0 if b == 0 else 32
                for tau in range(32):
                    src = SELB[:, :].rearrange(
                        "p (b e t) -> p b e t", b=BPC, e=EXP)[:, b, :, tau]
                    dst = PSbf[po:po + 8,
                               b * 4096 + tau * 128: b * 4096 + tau * 128 + 128]
                    p.add("tensor", lambda e, src=src, dst=dst: e.matmul(
                        dst, src, IDN[:, :],
                        start=True, stop=True, is_transpose=True),
                        waits=[v_sel] if (b == 0 and tau == 0) else [],
                        inc="tp" if (b == BPC - 1 and tau == 31) else None)
            t_selt = p.at("tp")
            for b in range(BPC):
                po = 0 if b == 0 else 32
                p.add("vector", lambda e, b=b, po=po: e.tensor_copy(
                    SELT[po:po + 8, :],
                    PSbf[po:po + 8, b * 4096:(b + 1) * 4096]),
                    waits=[t_selt] if b == 0 else [], inc="vp")
            v_st = p.at("vp")
            p.dma("gpsimd", sel_dr[0, :, :], SELT[0:8, :], "ssp", waits=[v_st])
            p.dma("gpsimd", sel_dr[1, :, :], SELT[32:40, :], "ssp")
            vph[("seldone",)] = v_st
            ssph[("selsp",)] = p.at("ssp")

        def plan_Cweights():
            w_t = [tph[("xnf_used", 2 * NG - 1)], vph[("kvdone", 1)]]
            p.dma("gpsimd", w1_b, w1_d, "sw", waits=w_t)
            s_w1 = p.at("sw")
            p.dma("gpsimd", wo_b, wo_d, "sw")
            p.dma("gpsimd", BOR, bo_d, "sw")
            p.dma("gpsimd", B2B, b2_d, "sw")
            p.dma("gpsimd", w2_b, w2_d, "sw")
            return s_w1, p.at("sw")

        def plan_C(b, ngroups=NG, hack_b2=False, w_selt1=None):
            w_all = [vph[("kvdone", 1)], ssp_A_end,
                     tph[("xnf_used", 2 * NG - 1)]]
            w_sel = [vph[("seldone",)], ssph[("selsp",)]]
            def plan_reload(g2):
                gg2 = b * NG + g2
                buf2 = gg2 % 2
                wn = [tph.get(("c_xnfq", gg2 - 2))]
                wq = [tph.get(("c_xnfq", gg2 - 2))]
                wmk = [vph.get(("c_mskq", gg2 - 2))]
                if gg2 == 0 or (b == 1 and g2 == 0):
                    wn = wn + w_all
                    wq = wq + w_all
                    wmk = wmk + w_all + w_sel
                p.dma("sync", XNF[buf2][:, :, :],
                      xn_sp[b, :, :, g2 * 512:(g2 + 1) * 512].transpose([1, 0, 2]),
                      f"sn{buf2}", waits=wn)
                p.dma("sync", QSL[buf2][:, :, :],
                      q_sp[b, :, :, g2 * 512:(g2 + 1) * 512].transpose([1, 0, 2]),
                      f"sq{buf2}", waits=wq)

            zv = {}
            def emit_denom(g2):
                gg2 = b * NG + g2
                buf2 = gg2 % 2
                w_q2 = (f"sq{buf2}", 16 * (gg2 // 2 + 1))
                if g2 == 0:
                    w0 = [vph.get(("c_z", gg2 - 1)), sph.get(("c_h", gg2 - 1)),
                          vph.get(("c_msk", gg2 - 1)), w_q2]
                    if gg2 == 0:
                        w0.append(p.at("vp"))
                else:
                    w0 = [vph[("c_att", (gg2 - 1, 6))], w_q2]
                for c in range(NC):
                    p.add("tensor", lambda e, c=c, b=b, buf2=buf2: e.matmul(
                        PS[0:16, 0:512], KST[:, b, c, :], QSL[buf2][:, c, :],
                        start=(c == 0), stop=(c == NC - 1)),
                        waits=w0 if c == 0 else [],
                        inc="tp" if c == NC - 1 else None)
                t_d = p.at("tp")
                p.add("vector", lambda e: e.tensor_scalar_add(
                    Z2[:, :], PS[0:16, 0:512], 1e-6), waits=[t_d])
                def _recip(e):
                    with nc.allow_low_precision(reason="Z to bf16 matmul operand"):
                        return e.reciprocal(ZBF[:, :], Z2[:, :])
                p.add("vector", _recip, inc="vp")
                vph[("c_z", gg2)] = p.at("vp")
                zv[g2] = p.at("vp")

            plan_reload(0)
            if ngroups > 1:
                plan_reload(1)
            emit_denom(0)
            for g in range(ngroups):
                gg = b * NG + g
                buf = gg % 2
                wmk = [vph.get(("c_msk", gg - 2))]
                if gg == 0 or (b == 1 and g == 0):
                    wmk = wmk + w_all + w_sel
                p.dma("scalar", MSK[buf][:, :, :],
                      sel_dr[b, :, g * 512:(g + 1) * 512].unsqueeze(0)
                      .broadcast_to([128, EXP, 512]), f"sm{buf}", waits=wmk)
                w_qrdy = (f"sq{buf}", 16 * (gg // 2 + 1))
                w_nrdy = (f"sn{buf}", 16 * (gg // 2 + 1))
                w_mrdy = (f"sm{buf}", 16 * (gg // 2 + 1))

                v_z = zv[g]

                base_ti = b * NT + g * NTG
                prev_par_tile = {0: None, 1: None}
                for tprev in (base_ti - 2, base_ti - 1):
                    if tprev >= 0:
                        prev_par_tile[tprev % 2] = tprev
                hfree = {}
                last_mask = None
                att_done = {}

                def emit_att(c):
                    ab = 1 if c % 2 == 0 else 2
                    zb = 0 if c % 2 == 0 else 3
                    w0 = [att_done.get(c - 2), v_z]
                    p.add("tensor", lambda e, ab=ab, c=c, b=b, buf=buf: e.matmul(
                        bank(ab), KVB[:, b, c, :], QSL[buf][:, c, :],
                        start=True, stop=True), waits=w0)
                    p.add("tensor", lambda e, zb=zb, c=c: e.matmul(
                        bank(zb), E16[:, c, :], ZBF[:, :],
                        start=True, stop=True), inc="tp")
                    t_a = p.at("tp")
                    zr = ZRS[c % 2]
                    p.add("vector", lambda e, zb=zb, zr=zr: e.tensor_copy(
                        zr, bank(zb)), waits=[t_a])
                    p.add("vector", lambda e, ab=ab, c=c, zr=zr: e.tensor_tensor(
                        ATT[:, c, :], bank(ab), zr, OP.mult), inc="vp")
                    att_done[c] = p.at("vp")
                    vph[("c_att", (gg, c))] = p.at("vp")

                def emit_w1(ei):
                    nonlocal last_mask
                    hb = (4, 5) if ei % 2 == 0 else (6, 7)
                    par = 0 if ei % 2 == 0 else 1
                    w0 = list(hfree.get(ei - 2, ()))
                    w0.append(w_nrdy)
                    if gg == 0 and ei in (0, 1):
                        w0.append(sw_w1)
                    if ei in (0, 1) and prev_par_tile[par] is not None:
                        w0.append(vph.get(("c_out", prev_par_tile[par])))
                    for hc in range(2):
                        for c in range(NC):
                            p.add("tensor", lambda e, hb=hb, hc=hc, c=c, ei=ei, buf=buf: e.matmul(
                                bank(hb[hc]), w1_b[:, c, ei, hc * 128:(hc + 1) * 128],
                                XNF[buf][:, c, :],
                                start=(c == 0), stop=(c == NC - 1)),
                                waits=w0 if (hc == 0 and c == 0) else [],
                                inc="tp" if (hc == 1 and c == NC - 1) else None)
                    t_h = p.at("tp")
                    for hc in range(2):
                        wg0 = [t_h] if hc == 0 else []
                        if last_mask is not None:
                            wg0 = wg0 + [last_mask]
                        p.add("scalar", lambda e, hb=hb, hc=hc, ei=ei: e.activation(
                            SCG, bank(hb[hc]), AF.Gelu,
                            bias=B1F[:, hc, ei:ei + 1]),
                            waits=wg0, inc="sp")
                        s_g = p.at("sp")
                        wm0 = [s_g]
                        if ei == 0 and hc == 0:
                            wm0 += [w_mrdy, tph.get(("c_xnf", gg - 1))]
                        p.add("vector", lambda e, ei=ei, hc=hc, buf=buf: e.tensor_tensor(
                            HSB[:, ei, hc, :], SCG, MSK[buf][:, ei, :],
                            OP.mult), waits=wm0, inc="vp")
                        last_mask = p.at("vp")
                    hfree[ei] = (p.at("vp"), p.at("sp"))

                emit_att(0)
                emit_att(1)
                for c in range(2, NC):
                    emit_att(c)
                    emit_w1(c - 2)
                emit_w1(6)
                emit_w1(7)
                v_att = p.at("vp")
                v_h = p.at("vp")
                sph[("c_h", gg)] = p.at("sp")
                vph[("c_msk", gg)] = v_h
                vph[("c_mskq", gg)] = v_h
                tph[("c_xnfq", gg)] = p.at("tp")
                if g + 1 < ngroups:
                    emit_denom(g + 1)
                if g + 2 < ngroups:
                    plan_reload(g + 2)

                for a in range(NTG):
                    tau = g * NTG + a
                    ti = b * NT + tau
                    par = ti % 2
                    fb = (4, 5) if par == 0 else (6, 7)
                    w0 = [vph.get(("c_out", ti - 2)), v_h]
                    if gg == 0 and a == 0:
                        w0.append(sw_all)
                    first = True
                    for c in range(NC):
                        for jh in range(2):
                            p.add("tensor", lambda e, fb=fb, c=c, jh=jh, a=a: e.matmul(
                                bank(fb[jh]), ATT[:, c, a * 128:(a + 1) * 128],
                                wo_b[:, c, jh * 512:(jh + 1) * 512],
                                start=(c == 0), stop=False),
                                waits=w0 if first else [])
                            first = False
                    for ei in range(EXP):
                        for hc in range(2):
                            for jh in range(2):
                                p.add("tensor", lambda e, fb=fb, ei=ei, hc=hc, jh=jh, a=a: e.matmul(
                                    bank(fb[jh]),
                                    HSB[:, ei, hc, a * 128:(a + 1) * 128],
                                    w2_b[:, hc, ei, jh * 512:(jh + 1) * 512],
                                    start=False, stop=False))
                    for jh in range(2):
                        wb2 = [w_selt1] if (w_selt1 is not None and g == 0
                                            and a == 0 and jh == 0) else []
                        p.add("tensor", lambda e, fb=fb, jh=jh, tau=tau: e.matmul(
                            bank(fb[jh]),
                            SELT[0:8, tau * 128:(tau + 1) * 128],
                            B2B[:, jh * 512:(jh + 1) * 512],
                            start=False, stop=False), waits=wb2)
                    for jh in range(2):
                        p.add("tensor", lambda e, fb=fb, jh=jh: e.matmul(
                            bank(fb[jh]), ONE1[:, :], BOR[:, jh * 512:(jh + 1) * 512],
                            start=False, stop=True),
                            inc="tp" if jh == 1 else None)
                    t_f = p.at("tp")
                    p.dma("scalar", XRS[par],
                          x[b, tau * 128:(tau + 1) * 128, :], f"sr{par}",
                          waits=[vph.get(("c_out", ti - 2))])
                    wev = [t_f, (f"sr{par}", 16 * (ti // 2 + 1)),
                           (f"so{par}", 16 * (ti // 2))]
                    p.add("vector", lambda e, fb=fb, par=par: e.tensor_tensor(
                        OUS[par],
                        PS[:, fb[0] * 512:(fb[0] + 2) * 512], XRS[par],
                        OP.add), waits=wev, inc="vp")
                    vph[("c_out", ti)] = p.at("vp")
                    p.dma("sync", out[b, tau * 128:(tau + 1) * 128, :],
                          OUS[par], f"so{par}",
                          waits=[vph[("c_out", ti)]])
                tph[("c_xnf", gg)] = p.at("tp")
                tph[("c_q", gg)] = p.at("tp")

        if phases == "pre":
            pass
        elif phases.startswith("s"):
            plan_A(0, ngroups=1, post=False, stage=float(phases[1:]))
        elif phases.startswith("g"):
            spec = phases[1:]
            ngroups = int(spec.rstrip("p"))
            plan_A(0, ngroups=ngroups, post=spec.endswith("p"))
            if spec.endswith("p"):
                plan_gatepost(0)
        else:
            plan_A(0)
            plan_A(1)
            plan_gatepost(0)
            plan_gatepost(1)
        ssp_A_end = ("ssp", p.cnt["ssp"])
        sw_w1 = sw_all = None
        if phases in ("at", "full", "fulld") or phases.startswith(("c", "d")):
            sw_w1, sw_all = plan_Cweights()
            plan_AR()
        if phases in ("atk", "full") and phases != "at":
            pass
        if phases == "at2":
            plan_topk()
        if phases in ("at", "at2") or phases.startswith("c"):
            pass
        if phases.startswith("c"):
            plan_topk()
            plan_C(0, ngroups=int(phases[1:]))
            p.add("sync", lambda e: e.nop(),
                  waits=[p.at("so0"), p.at("so1")])
        if phases.startswith("d"):
            plan_topk()
            plan_C(0)
            p.dma("gpsimd", SELT[0:8, :], sel_dr[1, :, :], "sw",
                  waits=[tph[("c_xnf", NG - 1)]])
            plan_C(1, ngroups=int(phases[1].rstrip("x")), w_selt1=p.at("sw"))
            p.add("sync", lambda e: e.nop(),
                  waits=[p.at("so0"), p.at("so1")])
        if phases in ("full", "fulld"):
            plan_topk()
            plan_C(0)
            p.dma("gpsimd", SELT[0:8, :], sel_dr[1, :, :], "sw",
                  waits=[tph[("c_xnf", NG - 1)]])
            plan_C(1, w_selt1=p.at("sw"))
            p.add("sync", lambda e: e.nop(),
                  waits=[p.at("so0"), p.at("so1")])
            if phases == "fulld":
                dbgs = {"KST": KST[:, :, :, :], "KVB": KVB[:, :, :, :],
                        "SELT": SELT[0:8, :], "ZBF": ZBF[:, :], "Z2": Z2[:, :],
                        "ATT": ATT[:, :, :], "HSB": HSB[:, :, :, :],
                        "QSL0": QSL[0][:, :, :], "XNF0": XNF[0][:, :, :],
                        "MSK0": MSK[0][:, :, :], "ST4": ST4[:, :], "STA": STA[:, :], "STB": STB[:, :], "XTM0": XTM[0][:, :], "SCMd": SCM, "SCRd": SCR}
                for nm, ap in dbgs.items():
                    dt = ap.dtype
                    shp = list(ap.shape)
                    od = nc.dram_tensor("dbg_" + nm, shp, dt,
                                        kind="ExternalOutput").ap()
                    p.dma("gpsimd", od, ap, "ssp",
                          waits=[p.at("so0"), p.at("so1")])
                p.add("gpsimd", lambda e: e.nop(), waits=[p.at("ssp")])
        elif not phases.startswith(("c", "d")):
            p.add("sync", lambda e: e.nop(),
                  waits=[p.at("ssp"), p.at("sw")])
            p.add("gpsimd", lambda e: e.nop(),
                  waits=[p.at("vp"), p.at("sp"), p.at("tp")])

        blk.sync(lambda e: p.emit("sync", e))
        blk.scalar(lambda e: p.emit("scalar", e))
        blk.vector(lambda e: p.emit("vector", e))
        blk.tensor(lambda e: p.emit("tensor", e))
        blk.gpsimd(lambda e: p.emit("gpsimd", e))

    return nc


_NC_CACHE = None


def _get_nc():
    global _NC_CACHE
    if _NC_CACHE is None:
        _NC_CACHE = build_nc()
    return _NC_CACHE


def make_in_maps(inputs):
    xf = np.ascontiguousarray(inputs["L0_attn_x"], dtype=np.float32)
    Wq = np.asarray(inputs["L0_attn_Wq"], dtype=np.float32)
    Wk = np.asarray(inputs["L0_attn_Wk"], dtype=np.float32)
    Wv = np.asarray(inputs["L0_attn_Wv"], dtype=np.float32)
    Wo = np.asarray(inputs["L0_attn_Wo"], dtype=np.float32)
    bq = np.asarray(inputs["L0_attn_bq"], dtype=np.float32)
    bk = np.asarray(inputs["L0_attn_bk"], dtype=np.float32)
    bv = np.asarray(inputs["L0_attn_bv"], dtype=np.float32)
    bo = np.asarray(inputs["L0_attn_bo"], dtype=np.float32)
    wg = np.asarray(inputs["L0_ffn_w_gate_w"], dtype=np.float32)
    bg = np.asarray(inputs["L0_ffn_w_gate_b"], dtype=np.float32)
    W1 = np.asarray(inputs["L0_ffn_W1"], dtype=np.float32)
    b1 = np.asarray(inputs["L0_ffn_b1"], dtype=np.float32)
    W2 = np.asarray(inputs["L0_ffn_W2"], dtype=np.float32)
    b2 = np.asarray(inputs["L0_ffn_b2"], dtype=np.float32)

    lay = {
        "wq_d": np.ascontiguousarray(Wq.T.reshape(NC, 128, D).transpose(1, 0, 2)),
        "wk_d": np.ascontiguousarray(Wk.T.reshape(NC, 128, D).transpose(1, 0, 2)),
        "wv_d": np.ascontiguousarray(Wv.T.reshape(NC, 128, D).transpose(1, 0, 2)),
        "wo_d": np.ascontiguousarray(Wo.T.reshape(NC, 128, D).transpose(1, 0, 2)),
        "w1_d": np.ascontiguousarray(
            W1.reshape(EXP, NC, 128, 256).transpose(2, 1, 0, 3)),
        "w2_d": np.ascontiguousarray(
            W2.reshape(EXP, 2, 128, D).transpose(2, 1, 0, 3)),
        "bqf_d": np.ascontiguousarray(bq.reshape(NC, 128).T),
        "bqr_d": bq.reshape(1, D),
        "b1f_d": np.ascontiguousarray(b1.reshape(EXP, 2, 128).transpose(2, 1, 0)),
        "bk_d": bk.reshape(1, D),
        "bv_d": bv.reshape(1, D),
        "bo_d": bo.reshape(1, D),
        "b2_d": np.ascontiguousarray(b2),
        "wg_d": np.ascontiguousarray(wg.reshape(NC, 128, EXP).transpose(1, 0, 2)),
        "bg_d": bg.reshape(1, EXP),
        "tri_d": np.triu(np.ones((128, 128), dtype=np.float32), 1),
        "id_d": np.eye(128, dtype=np.float32),
        "onq_d": np.ones((128, 128), dtype=np.float32),
        "ck4_d": np.tile(np.arange(1, 5, dtype=np.float32) / 5.0,
                         16).reshape(1, 64),
    }
    e16 = np.zeros((16, 8, 128), dtype=np.float32)
    for c in range(NC):
        e16[2 * c, c, 0:64] = 1.0
        e16[2 * c + 1, c, 64:128] = 1.0
    lay["e16_d"] = e16
    hm = np.zeros((128, NC, H), dtype=np.float32)
    for c in range(NC):
        hm[0:64, c, 2 * c] = 1.0
        hm[64:128, c, 2 * c + 1] = 1.0
    lay["hm_d"] = hm

    in_maps = []
    for i in range(NCORES):
        m = dict(lay)
        m["x"] = np.ascontiguousarray(xf[i * BPC:(i + 1) * BPC])
        in_maps.append(m)
    return in_maps


def kernel(**inputs):
    in_maps = make_in_maps(inputs)
    nc = _get_nc()
    res = run_bass_kernel_spmd(nc, in_maps, core_ids=list(range(NCORES)))
    return np.concatenate([np.asarray(res.results[i]["out"])
                           for i in range(NCORES)], axis=0)
